# version 01
# baseline (speedup 1.0000x reference)
import numpy as np
import ml_dtypes
import contextlib
import concourse.bass as bass
import concourse.mybir as mybir
from concourse.bass_utils import run_bass_kernel_spmd

F32 = mybir.dt.float32
BF16 = mybir.dt.bfloat16
I32 = mybir.dt.int32
Alu = mybir.AluOpType
Act = mybir.ActivationFunctionType
AX = mybir.AxisListType

bf16 = ml_dtypes.bfloat16

T = 4096
TH = 2048
D = 1024
COLS = 1536
H8 = 8
DH = 64
F = 2049
FPAD = 2176
NSEL = 64
DHID = 4096
NIT_TOPK = 30

O_WQ = 57344
O_Q = 81920
O_K = 114688
O_V = 147456
O_SP = 180224

def _u2f():
    u = np.arange(FPAD)
    f = np.where(u < 1152, u, 2048 - (u - 1152))
    valid = (u <= 1024) | (u >= 1152)
    return f, valid

_TBL = {}

def tables():
    if _TBL:
        return _TBL
    f_of_u, valid = _u2f()
    fv = f_of_u.astype(np.float64)
    tp = np.arange(TH, dtype=np.float64)
    fp = np.arange(1152, dtype=np.float64)
    ang2 = 2.0 * np.pi / TH
    C2 = np.cos(ang2 * np.outer(tp, fp))
    S2n = -np.sin(ang2 * np.outer(tp, fp))
    C2[:, 1025:] = 0.0
    S2n[:, 1025:] = 0.0
    def tile_tf(M, nchunks):
        nm = M.shape[0] // 128
        out = np.zeros((nchunks, M.shape[0], 128), M.dtype)
        for j in range(nchunks):
            blk = M[:, j * 128:(j + 1) * 128].reshape(nm, 128, 128)
            out[j] = blk.transpose(1, 0, 2).reshape(M.shape[0], 128)
        return out
    _TBL["C2"] = tile_tf(C2.astype(np.float32), 9)
    _TBL["S2n"] = tile_tf(S2n.astype(np.float32), 9)
    fpr = np.arange(1152, dtype=np.float64)
    th = 2.0 * np.pi * fpr / 4096.0
    twc = np.cos(th); tws = np.sin(th)
    twc[1025:] = 0.0; tws[1025:] = 0.0
    _TBL["twc"] = np.ascontiguousarray(twc.reshape(9, 128).T).astype(np.float32)
    _TBL["tws"] = np.ascontiguousarray(tws.reshape(9, 128).T).astype(np.float32)
    _TBL["twns"] = np.ascontiguousarray((-tws).reshape(9, 128).T).astype(np.float32)
    fi = np.where(valid, f_of_u, -1)
    a = np.where(valid, 16 * (f_of_u // 16), -1).astype(np.float64)
    b = np.where(valid, f_of_u % 16, 0).astype(np.float64)
    fa = np.ascontiguousarray(a.reshape(17, 128).T).astype(np.float32)
    fb = np.ascontiguousarray(b.reshape(17, 128).T).astype(np.float32)
    fab = np.zeros((128, 34), np.float32)
    fab[:, 0::2] = fa; fab[:, 1::2] = fb
    _TBL["fioAB"] = fab
    t1 = np.arange(64, dtype=np.float64)
    ang = 2.0 * np.pi / 4096.0
    A1 = np.outer(fv, t1) * (2.0 * np.pi / 64.0)
    B1 = np.outer(fv, t1) * ang
    z = valid[:, None]
    _TBL["A1c"] = np.where(z, np.cos(A1), 0).astype(np.float32)
    _TBL["A1s"] = np.where(z, np.sin(A1), 0).astype(np.float32)
    _TBL["B1c"] = np.where(z, np.cos(B1), 0).astype(np.float32)
    _TBL["B1s"] = np.where(z, np.sin(B1), 0).astype(np.float32)
    t1p = np.arange(32, dtype=np.float64)
    t0p = np.arange(64, dtype=np.float64)
    A2 = np.outer(fv, t1p) * (2.0 * np.pi / 32.0)
    B2e = np.outer(fv, t0p) * ang2
    B2o = B2e + (fv * ang)[:, None]
    _TBL["A2c"] = np.where(z, np.cos(A2), 0).astype(np.float32)
    _TBL["A2s"] = np.where(z, np.sin(A2), 0).astype(np.float32)
    _TBL["B2ec"] = np.where(z, np.cos(B2e), 0).astype(np.float32)
    _TBL["B2es"] = np.where(z, np.sin(B2e), 0).astype(np.float32)
    _TBL["B2oc"] = np.where(z, np.cos(B2o), 0).astype(np.float32)
    _TBL["B2os"] = np.where(z, np.sin(B2o), 0).astype(np.float32)
    t = np.arange(T, dtype=np.float64)
    fr = np.arange(F, dtype=np.float64)
    w = np.full(F, 2.0); w[0] = 1.0; w[-1] = 1.0
    CC = np.cos(ang * np.outer(t, fr))
    SS = np.sin(ang * np.outer(t, fr))
    _TBL["cvec"] = (CC * (w / T)).sum(axis=1).astype(np.float32)[None, :]
    _TBL["svec"] = (-(SS * (w / T)).sum(axis=1)).astype(np.float32)[None, :]
    LT = (np.arange(FPAD)[:, None] <= np.arange(FPAD)[None, :])
    _TBL["LT"] = tile_tf(LT.astype(np.float32), 17).astype(bf16)
    _TBL["jp1"] = np.tile(np.arange(1, NSEL + 1, dtype=np.float32)[None, :],
                          (128, H8))
    _TBL["ident"] = np.eye(128, dtype=np.float32).astype(bf16)
    _TBL["identf"] = np.eye(128, dtype=np.float32)
    KRfull = np.concatenate([_TBL["A1c"], _TBL["A1s"], _TBL["B1c"], _TBL["B1s"],
                             _TBL["A2c"], _TBL["A2s"], _TBL["B2ec"], _TBL["B2es"],
                             _TBL["B2oc"], _TBL["B2os"]], axis=1)
    _TBL["KR"] = np.ascontiguousarray(KRfull.reshape(17, 128, 576))
    _TBL["cvsv"] = np.concatenate([_TBL["cvec"], _TBL["svec"]], axis=0)
    zr = np.zeros_like(_TBL["cvec"])
    _TBL["cva"] = np.concatenate([_TBL["cvec"], zr], axis=0)
    _TBL["cvb"] = np.concatenate([zr, _TBL["svec"]], axis=0)
    qmw = np.zeros((128, 100), np.float64)
    pp = np.arange(128, dtype=np.float64)
    for m in range(16):
        qmw[:, 2 * m] = 1.0
    for m in range(16, 32):
        tt = 2.0 * (128 * (m - 16) + pp) + 1.0
        wim = -1.0 / np.tan(np.pi * tt / T)
        w_hi = wim.astype(bf16).astype(np.float64)
        qmw[:, 2 * m + 1] = w_hi
        qmw[:, 66 + 2 * (m - 16) + 1] = wim - w_hi
    qmw[0, 64] = 2048.0
    _TBL["qmw"] = qmw.astype(bf16)
    return _TBL


class Prog:
    def __init__(self, debug=()):
        self.debug = set(debug)
        import os as _os0
        self.nc = bass.Bass("TRN2", target_bir_lowering=False,
                            detect_race_conditions=False,
                            debug=bool(_os0.environ.get("MKDBG")))
        self.es = contextlib.ExitStack()
        self.dram = {}
        self.sb = {}
        self.ps = {}
        self.sems = {}
        self.cnt = {}

    def din(self, name, shape, dt=F32):
        ap = self.nc.dram_tensor(name, list(shape), dt, kind="ExternalInput").ap()
        self.dram[name] = ap
        return ap

    def dout(self, name, shape, dt=F32):
        ap = self.nc.dram_tensor(name, list(shape), dt, kind="ExternalOutput").ap()
        self.dram[name] = ap
        return ap

    def dtmp(self, name, shape, dt=F32):
        ap = self.nc.dram_tensor(name, list(shape), dt).ap()
        self.dram[name] = ap
        return ap

    def at(self, name, shape, dt, off):
        h = self.nc.alloc_sbuf_tensor_at(name, list(shape), dt, offset=off)
        self.sb[name] = h
        return h

    def sbuf(self, name, shape, dt=F32, scope=None):
        side = "left" if scope is not None else "right"
        h = (scope or self.es).enter_context(
            self.nc.sbuf_tensor(name, list(shape), dt, side=side))
        self.sb[name] = h
        return h

    def psum(self, name, shape, dt=F32, scope=None):
        h = (scope or self.es).enter_context(self.nc.psum_tensor(name, list(shape), dt))
        self.ps[name] = h
        return h

    def sem(self, name):
        assert name not in self.sems, name
        h = self.es.enter_context(self.nc.semaphore(name))
        self.sems[name] = h
        self.cnt[name] = 0
        return h

    def inc(self, instr, name, k=1):
        instr.then_inc(self.sems[name], k)
        self.cnt[name] += k
        return self.cnt[name]

    def n(self, name):
        return self.cnt[name]


def stage_core(inputs, core):
    b, half = core // 2, core % 2
    x = np.asarray(inputs["L0_attn_query"], dtype=np.float32)
    tb = tables()
    d = {}
    d["xb"] = np.ascontiguousarray(x[b])
    xT = x[b].T
    perm = np.concatenate([np.arange(0, T, 2), np.arange(1, T, 2)])
    xTeo = xT[:, perm]
    xtt = np.zeros((32, 1024, 128), np.float32)
    blkv = xTeo.reshape(8, 128, 32, 128)
    xtt[:] = blkv.transpose(2, 1, 0, 3).reshape(32, 1024, 128)
    d["xTt"] = np.ascontiguousarray(xtt)
    cs = slice(half * 512, half * 512 + 512)
    d["wqkv"] = np.ascontiguousarray(
        np.concatenate([np.asarray(inputs["L0_attn_Wq"], np.float32)[:, cs],
                        np.asarray(inputs["L0_attn_Wk"], np.float32)[:, cs],
                        np.asarray(inputs["L0_attn_Wv"], np.float32)[:, cs]], axis=1))
    d["bqkv"] = np.ascontiguousarray(
        np.concatenate([np.asarray(inputs["L0_attn_bq"], np.float32)[cs],
                        np.asarray(inputs["L0_attn_bk"], np.float32)[cs],
                        np.asarray(inputs["L0_attn_bv"], np.float32)[cs]]))[None, :]
    for k in ["C2", "S2n", "twc", "tws", "twns", "fioAB", "LT", "jp1",
              "KR", "cvsv", "cva", "cvb", "ident", "identf", "qmw"]:
        d[k] = tb[k]
    d["wo"] = np.ascontiguousarray(np.asarray(inputs["L0_attn_Wo"], np.float32))
    d["bo"] = np.ascontiguousarray(np.asarray(inputs["L0_attn_bo"], np.float32))[None, :]
    ch = slice(half * 2048, half * 2048 + 2048)
    d["w1t"] = np.ascontiguousarray(
        np.asarray(inputs["L0_ffn_w1"], np.float32)[ch, :].T)
    d["w2t"] = np.ascontiguousarray(
        np.asarray(inputs["L0_ffn_w2"], np.float32)[:, ch].T)
    cw = np.asarray(inputs["L0_ffn_conv_w"], np.float32)[ch, 0, :]
    d["cwh"] = np.ascontiguousarray(
        cw.reshape(16, 128, 3).transpose(1, 0, 2).reshape(128, 48))
    cb = np.asarray(inputs["L0_ffn_conv_b"], np.float32)[ch]
    d["cbh"] = np.ascontiguousarray(cb.reshape(16, 128).T)
    return d


def emit_ln_proj(P, blk, scope):
    nc = P.nc
    xb, xTt, wqkv = P.dram["xb"], P.dram["xTt"], P.dram["wqkv"]
    wq_sb = P.at("wq_sb", (128, 8 * COLS), BF16, O_WQ)
    q_sb = P.at("q_sb", (128, 32 * 512), BF16, O_Q)
    k_sb = P.at("k_sb", (128, 32 * 512), BF16, O_K)
    v_sb = P.at("v_sb", (128, 32 * 512), BF16, O_V)
    pan = [P.sbuf(f"pan{i}", (128, 8 * 128), BF16, scope) for i in range(2)]
    xch = [P.sbuf(f"xch{i}", (128, D), F32, scope) for i in range(2)]
    st_all = P.sbuf("st_all", (128, 12 * 32), F32, scope)
    mv_all = P.sbuf("mv_all", (128, 4 * 32), F32, scope)
    u0c = P.sbuf("u0c", (128, 32), F32, scope)
    u1c = P.sbuf("u1c", (128, 32), F32, scope)
    u2c = P.sbuf("u2c", (128, 32), F32, scope)
    u3c = P.sbuf("u3c", (128, 32), F32, scope)
    tcol = P.sbuf("tcol", (128, 32), F32, scope)
    lcol = P.sbuf("lcol", (128, 32), F32, scope)
    rcol = P.sbuf("rcol", (128, 32))
    mrow = P.sbuf("mrow", (1, T), BF16, scope)
    onew = P.sbuf("onew", (1, COLS), BF16, scope)
    ones_col = P.sbuf("ones_col", (128, 1), BF16)
    pp = [P.psum(f"pp{i}", (128, 512), F32, scope) for i in range(6)]
    pone = P.psum("pone", (1, 512), F32, scope)
    pm = P.psum("pm", (1, 128), F32, scope)

    for s in ["ldx0", "ldx1", "pan0", "pan1", "ldwq", "mst", "rst", "mrowd",
              "onewd", "mm", "evac", "onescol", "vard"]:
        P.sem(s)

    @blk.sync
    def _(sy):
        for i in range(32):
            if i >= 2:
                sy.wait_ge(P.sems["mst"], i - 1)
            s = "ldx%d" % (i % 2)
            if i < 16:
                srcx = xb[256 * i:256 * (i + 1):2, :]
            else:
                srcx = xb[256 * (i - 16) + 1:256 * (i - 15):2, :]
            sy.dma_start(out=xch[i % 2][:], in_=srcx).then_inc(P.sems[s], 16)
            P.cnt[s] += 16

    @blk.gpsimd
    def _(g):
        for k in range(8):
            g.dma_start(out=wq_sb[:, k * COLS:(k + 1) * COLS],
                        in_=wqkv[k * 128:(k + 1) * 128, :]).then_inc(P.sems["ldwq"], 16)
            P.cnt["ldwq"] += 16
        for m in range(32):
            s = "pan%d" % (m % 2)
            if m >= 2:
                g.wait_ge(P.sems["mm"], 3 + 4 * (m - 1))
            g.dma_start(out=pan[m % 2][:].rearrange("p (k t) -> p k t", t=128),
                        in_=xTt[m]).then_inc(P.sems[s], 16)
            P.cnt[s] += 16

    mvr = mv_all[:].rearrange("p (c f) -> p c f", f=4)

    @blk.vector
    def _(v):
        P.inc(v.memset(ones_col[:], 1.0), "onescol", 1)
        for i in range(32):
            v.wait_ge(P.sems["ldx%d" % (i % 2)], 16 * (i // 2 + 1))
            xc = xch[i % 2]
            v.bn_stats(st_all[:, 12 * i:12 * i + 6], xc[:, 0:512])
            P.inc(v.bn_stats(st_all[:, 12 * i + 6:12 * i + 12], xc[:, 512:1024]),
                  "mst", 1)
        v.drain()
        for i in range(32):
            v.bn_aggr(mv_all[:, 4 * i:4 * i + 2], st_all[:, 12 * i:12 * i + 6])
            v.bn_aggr(mv_all[:, 4 * i + 2:4 * i + 4], st_all[:, 12 * i + 6:12 * i + 12])
        v.drain()
        v.tensor_tensor(u0c[:], mvr[:, :, 0:1], mvr[:, :, 2:3], Alu.subtract)
        v.tensor_tensor(u1c[:], mvr[:, :, 1:2], mvr[:, :, 3:4], Alu.add)
        v.drain()
        v.tensor_tensor(u2c[:], u0c[:], u0c[:], Alu.mult)
        v.tensor_scalar(u3c[:], u1c[:], 0.5, 1e-5, Alu.mult, Alu.add)
        v.drain()
        P.inc(v.scalar_tensor_tensor(tcol[:], u2c[:], 0.25, u3c[:], Alu.mult, Alu.add),
              "vard", 1)

    @blk.scalar
    def _(s):
        s.wait_ge(P.sems["vard"], 1)
        s.activation(lcol[:], tcol[:], Act.Ln)
        s.drain()
        P.inc(s.activation(rcol[:], lcol[:], Act.Exp, scale=-0.5), "rst", 32)

    onew_mm_base = P.n("mm")

    @blk.tensor
    def _(te):
        te.wait_ge(P.sems["ldwq"], 16 * 8)
        te.wait_ge(P.sems["onescol"], 1)
        for part in range(3):
            if part > 0:
                te.wait_ge(P.sems["onewd"], part)
            for k in range(8):
                mm = te.matmul(pone[:], ones_col[:],
                               wq_sb[:, k * COLS + part * 512: k * COLS + part * 512 + 512],
                               start=(k == 0), stop=(k == 7))
            P.inc(mm, "mm", 1)

    @blk.scalar
    def _(s):
        for part in range(3):
            s.wait_ge(P.sems["mm"], onew_mm_base + part + 1)
            P.inc(s.activation(onew[0:1, part * 512:(part + 1) * 512], pone[0:1, :],
                               Act.Copy, scale=-1.0), "onewd", 1)

    proj_mm_base = P.n("mm")

    @blk.tensor
    def _(te):
        te.wait_ge(P.sems["onewd"], 3)
        for m in range(32):
            te.wait_ge(P.sems["pan%d" % (m % 2)], 16 * (m // 2 + 1))
            p = pan[m % 2]
            if m >= 1:
                te.wait_ge(P.sems["mrowd"], m)
            for k in range(8):
                mm = te.matmul(pm[:], ones_col[:], p[:, k * 128:(k + 1) * 128],
                               start=(k == 0), stop=(k == 7))
            P.inc(mm, "mm", 1)
            te.wait_ge(P.sems["mrowd"], m + 1)
            for part in range(3):
                bank = pp[(m % 2) * 3 + part]
                if m >= 2:
                    te.wait_ge(P.sems["evac"], 3 * (m - 2) + part + 1)
                for k in range(8):
                    te.matmul(bank[:], p[:, k * 128:(k + 1) * 128],
                              wq_sb[:, k * COLS + part * 512: k * COLS + part * 512 + 512],
                              start=(k == 0), stop=False)
                mm = te.matmul(bank[:], mrow[0:1, m * 128:(m + 1) * 128],
                               onew[0:1, part * 512:(part + 1) * 512],
                               start=False, stop=True)
                P.inc(mm, "mm", 1)

    @blk.vector
    def _(v):
        for m in range(32):
            v.wait_ge(P.sems["mm"], proj_mm_base + 4 * m + 1)
            P.inc(v.tensor_scalar(mrow[0:1, m * 128:(m + 1) * 128], pm[0:1, :],
                                  1.0 / D, None, Alu.mult), "mrowd", 1)

    @blk.scalar
    def _(s):
        dst = [q_sb, k_sb, v_sb]
        s.wait_ge(P.sems["rst"], 32)
        for m in range(32):
            for part in range(3):
                s.wait_ge(P.sems["mm"], proj_mm_base + 4 * m + part + 2)
                bank = pp[(m % 2) * 3 + part]
                a = s.activation(dst[part][:, m * 512:(m + 1) * 512], bank[:],
                                 Act.Copy, scale=rcol[:, m:m + 1])
                P.inc(a, "evac", 1)

    return dict(q_sb=q_sb, k_sb=k_sb, v_sb=v_sb)


def emit_dft(P, blk, scope, q_sb):
    nc = P.nc
    C2, S2n = P.dram["C2"], P.dram["S2n"]
    ct = [P.at(f"ct{i}", (128, 16 * 128), BF16, O_WQ + 4096 * i) for i in range(2)]
    snt = [P.at(f"snt{i}", (128, 16 * 128), BF16, O_WQ + 8192 + 4096 * i)
           for i in range(2)]
    twc = P.sbuf("twc_t", (128, 9), F32, scope)
    tws = P.sbuf("tws_t", (128, 9), F32, scope)
    twns = P.sbuf("twns_t", (128, 9), F32, scope)
    bqrow = P.sbuf("bqrow", (1, COLS))
    Qre = P.at("Qre", (128, 17 * 512), BF16, O_SP)
    Qim = P.at("Qim", (128, 17 * 512), BF16, O_SP + 17408)
    energy = P.sbuf("energy", (128, 17 * 8))
    tb1 = P.sbuf("tb1", (128, 512), F32, scope)
    tb2 = P.sbuf("tb2", (128, 512), F32, scope)
    tb3 = P.sbuf("tb3", (128, 512), F32, scope)
    tb4 = P.sbuf("tb4", (128, 512), F32, scope)
    e2 = [P.sbuf(f"e2_{i}", (128, 512), F32, scope) for i in range(2)]
    sqout = P.sbuf("sqout", (128, 512), F32, scope)
    rowt = P.sbuf("rowt", (1, 512), F32, scope)
    psE = [P.psum(f"psE{i}", (128, 512), F32, scope) for i in range(8)]

    for s in ["tld0", "tld1", "tldtw", "dmm", "bfly", "e2s", "esq"]:
        P.sem(s)

    @blk.gpsimd
    def _(g):
        g.wait_ge(P.sems["evac"], 96)
        g.dma_start(out=twc[:], in_=P.dram["twc"]).then_inc(P.sems["tldtw"], 16)
        g.dma_start(out=tws[:], in_=P.dram["tws"]).then_inc(P.sems["tldtw"], 16)
        g.dma_start(out=twns[:], in_=P.dram["twns"]).then_inc(P.sems["tldtw"], 16)
        g.dma_start(out=bqrow[:], in_=P.dram["bqkv"]).then_inc(P.sems["tldtw"], 16)
        P.cnt["tldtw"] += 64
        for j in range(9):
            sname = "tld%d" % (j % 2)
            if j >= 2:
                g.wait_ge(P.sems["bfly"], j - 1)
            g.dma_start(out=ct[j % 2][:].rearrange("p (m f) -> p m f", f=128),
                        in_=C2[j]).then_inc(P.sems[sname], 16)
            g.dma_start(out=snt[j % 2][:].rearrange("p (m f) -> p m f", f=128),
                        in_=S2n[j]).then_inc(P.sems[sname], 16)
            P.cnt[sname] += 32

    @blk.tensor
    def _(te):
        for j in range(9):
            te.wait_ge(P.sems["tld%d" % (j % 2)], 32 * (j // 2 + 1))
            if j >= 2:
                te.wait_ge(P.sems["bfly"], j - 1)
            ps = psE[(j % 2) * 4:(j % 2) * 4 + 4]
            for part, (tbl, rbase) in enumerate(
                    [(ct, 0), (snt, 0), (ct, 16), (snt, 16)]):
                for m in range(16):
                    mm = te.matmul(ps[part][:],
                                   tbl[j % 2][:, m * 128:(m + 1) * 128],
                                   q_sb[:, (rbase + m) * 512:(rbase + m + 1) * 512],
                                   start=(m == 0), stop=(m == 15))
                P.inc(mm, "dmm", 1)

    @blk.vector
    def _(v):
        v.wait_ge(P.sems["tldtw"], 64)
        for j in range(9):
            v.wait_ge(P.sems["dmm"], 4 * (j + 1))
            Ere, Eim, Ore, Oim = psE[(j % 2) * 4:(j % 2) * 4 + 4]
            v.tensor_scalar(tb1[:], Ore[:], twc[:, j:j + 1], None, Alu.mult)
            v.scalar_tensor_tensor(tb2[:], Oim[:], tws[:, j:j + 1], tb1[:],
                                   Alu.mult, Alu.add)
            v.tensor_scalar(tb3[:], Oim[:], twc[:, j:j + 1], None, Alu.mult)
            v.scalar_tensor_tensor(tb4[:], Ore[:], twns[:, j:j + 1], tb3[:],
                                   Alu.mult, Alu.add)
            v.tensor_tensor(Qre[:, j * 512:(j + 1) * 512], Ere[:], tb2[:], Alu.add)
            if j < 8:
                v.tensor_tensor(Qim[:, j * 512:(j + 1) * 512], Eim[:], tb4[:], Alu.add)
                v.tensor_tensor(Qre[:, (9 + j) * 512:(10 + j) * 512], Ere[:], tb2[:],
                                Alu.subtract)
                r = v.tensor_tensor(Qim[:, (9 + j) * 512:(10 + j) * 512], tb4[:],
                                    Eim[:], Alu.subtract)
            else:
                r = v.tensor_tensor(Qim[:, j * 512:(j + 1) * 512], Eim[:], tb4[:],
                                    Alu.add)
            P.inc(r, "bfly", 1)
        v.drain()
        v.scalar_tensor_tensor(rowt[0:1, 0:512], bqrow[0:1, 0:512], 4096.0,
                               Qre[0:1, 0:512], Alu.mult, Alu.add)
        v.drain()
        P.inc(v.tensor_copy(Qre[0:1, 0:512], rowt[0:1, 0:512]), "bfly", 1)

    @blk.vector
    def _(v):
        v.wait_ge(P.sems["bfly"], 10)
        for j in range(17):
            if j >= 2:
                v.wait_ge(P.sems["esq"], 8 * (j - 1))
            v.tensor_tensor(tb1[:], Qre[:, j * 512:(j + 1) * 512],
                            Qre[:, j * 512:(j + 1) * 512], Alu.mult)
            v.tensor_tensor(tb2[:], Qim[:, j * 512:(j + 1) * 512],
                            Qim[:, j * 512:(j + 1) * 512], Alu.mult)
            P.inc(v.tensor_tensor(e2[j % 2][:], tb1[:], tb2[:], Alu.add), "e2s", 1)

    @blk.scalar
    def _(s):
        for j in range(17):
            s.wait_ge(P.sems["e2s"], j + 1)
            for h in range(8):
                P.inc(s.activation(sqout[:, h * 64:(h + 1) * 64],
                                   e2[j % 2][:, h * 64:(h + 1) * 64], Act.Sqrt,
                                   accum_out=energy[:, j * 8 + h:j * 8 + h + 1]),
                      "esq", 1)

    return dict(Qre=Qre, Qim=Qim, energy=energy, bqrow=bqrow)


def emit_topk(P, blk, scope, energy, q_sb):
    nc = P.nc
    fioAB = P.sbuf("fioAB_t", (128, 34), BF16)
    jp1 = P.sbuf("jp1t", (128, 512))
    qmw = P.sbuf("qmw_t", (128, 100), BF16, scope)
    qm2 = P.sbuf("qm2", (2, 512), BF16)
    qm2f = P.sbuf("qm2f", (2, 512), F32, scope)
    biasr2 = P.sbuf("biasr2", (2, 512), F32, scope)
    onesrow_f = P.sbuf("onesrow_f", (1, 128), F32, scope)
    onescol_f = P.sbuf("onescol_f", (128, 1), F32, scope)
    emax = P.sbuf("emax", (128, 2), F32, scope)
    rows = P.sbuf("rows", (1, 14 * 8), F32, scope)
    cmpb = P.sbuf("cmpb", (128, 136), BF16, scope)
    partials = P.sbuf("partials", (128, 8), F32, scope)
    maskb = P.sbuf("maskb", (128, 17 * 8), BF16)
    Ptmp = P.sbuf("Ptmp", (128, 512), BF16, scope)
    ltst = [P.at(f"ltst{i}", (128, 17 * 128), BF16, O_WQ + 4352 * i)
            for i in range(2)]
    rank_sb = P.sbuf("rank_sb", (128, 17 * 8))
    P_sb = P.at("P_sb", (128, 17 * 512), BF16, O_Q)
    fsel = P.sbuf("fsel", (128, 4))
    ftmp = P.sbuf("ftmp", (128, 8), F32, scope)
    pthr = P.psum("pthr", (128, 136), F32, scope)
    pcnt = P.psum("pcnt", (1, 8), F32, scope)
    prank = [P.psum(f"prank{i}", (128, 8), F32, scope) for i in range(2)]
    pfs = P.psum("pfs", (128, 8), F32, scope)
    pqm2 = P.psum("pqm2", (2, 512), F32, scope)

    for s in ["cld", "emx", "emxg", "thr", "cmp", "cnt", "cntv", "msk",
              "ltld0", "ltld1", "rnk", "rnkev", "pbld", "fsl", "fslv",
              "qmm2", "qmev"]:
        P.sem(s)

    qm_mms = [(2 * m, m) for m in range(32)] + [(64, 0)] \
        + [(66 + 2 * i, 16 + i) for i in range(16)]

    SLOT = dict(lo0=0, hi0=1, th0=2, lo1=3, hi1=4, th1=5, cnt=6, m=7, mi=8,
                t1=9, t2=10, sm=11, c64=12)

    def rw(name):
        i = SLOT[name]
        return rows[0:1, i * 8:(i + 1) * 8]

    def lo(b): return rw("lo1" if b else "lo0")
    def hi(b): return rw("hi1" if b else "hi0")
    def th(b): return rw("th1" if b else "th0")

    @blk.gpsimd
    def _(g):
        g.wait_ge(P.sems["esq"], 136)
        g.dma_start(out=fioAB[:], in_=P.dram["fioAB"]).then_inc(P.sems["cld"], 16)
        g.dma_start(out=jp1[:], in_=P.dram["jp1"]).then_inc(P.sems["cld"], 16)
        g.dma_start(out=qmw[:], in_=P.dram["qmw"]).then_inc(P.sems["cld"], 16)
        P.cnt["cld"] += 48
        g.wait_ge(P.sems["emx"], 1)
        P.inc(g.tensor_reduce(emax[0:1, 1:2], emax[:, 0:1], axis=AX.C, op=Alu.max),
              "emxg", 1)

    @blk.vector
    def _(v):
        v.wait_ge(P.sems["esq"], 136)
        P.inc(v.tensor_reduce(emax[:, 0:1], energy[:], axis=AX.X, op=Alu.max),
              "emx", 1)
        v.memset(rows[:], 0.0)
        v.memset(onesrow_f[:], 1.0)
        v.memset(onescol_f[:], 1.0)
        v.wait_ge(P.sems["emxg"], 1)
        v.drain()
        hi0 = bass.AP(emax, 1, [[2, 1], [0, 8]])
        v.tensor_scalar(hi(0), hi0, 1.0, None, Alu.add)
        v.memset(rw("c64"), 64.0)
        v.drain()
        P.inc(v.tensor_scalar(th(0), hi(0), 0.5, None, Alu.mult), "thr", 1)

    cmr = cmpb[:].rearrange("p (c h) -> p h c", h=8)

    for it in range(NIT_TOPK + 1):
        final = (it == NIT_TOPK)
        A, B = it % 2, 1 - it % 2

        @blk.tensor
        def _(te, it=it, A=A, final=final):
            te.wait_ge(P.sems["thr"], it + 1)
            src = lo(A) if final else th(A)
            thr136 = bass.AP(src.tensor, src.offset, [[112, 1], [0, 17], [1, 8]])
            P.inc(te.matmul(pthr[:], onesrow_f[:], thr136, start=True, stop=True),
                  "cmp", 1)
            if not final:
                te.wait_ge(P.sems["cntv"], it + 1)
                P.inc(te.matmul(pcnt[:], onescol_f[:], partials[:],
                                start=True, stop=True), "cnt", 1)
            if it == 0:
                te.wait_ge(P.sems["cld"], 48)
            for idx in range(2 * it, min(2 * it + 2, len(qm_mms))):
                c, mch = qm_mms[idx]
                mm = te.matmul(pqm2[:], qmw[:, c:c + 2],
                               q_sb[:, mch * 512:(mch + 1) * 512],
                               start=(idx == 0), stop=(idx == len(qm_mms) - 1))
                if idx == len(qm_mms) - 1:
                    P.inc(mm, "qmm2", 1)

        @blk.vector
        def _(v, it=it, A=A, B=B, final=final):
            v.wait_ge(P.sems["cmp"], it + 1)
            if final:
                P.inc(v.tensor_tensor(maskb[:], energy[:], pthr[:], Alu.is_ge),
                      "msk", 1)
                return
            v.tensor_tensor(cmpb[:], energy[:], pthr[:], Alu.is_ge)
            v.drain()
            P.inc(v.tensor_reduce(partials[:], cmr, axis=AX.X, op=Alu.add), "cntv", 1)
            v.wait_ge(P.sems["cnt"], it + 1)
            v.tensor_copy(rw("cnt"), pcnt[0:1, :])
            v.drain()
            v.tensor_tensor(rw("m"), rw("cnt"), rw("c64"), Alu.is_ge)
            v.tensor_tensor(rw("t1"), th(A), lo(A), Alu.subtract)
            v.tensor_tensor(rw("t2"), th(A), hi(A), Alu.subtract)
            v.drain()
            v.tensor_scalar(rw("mi"), rw("m"), -1.0, 1.0, Alu.mult, Alu.add)
            v.tensor_tensor(rw("sm"), rw("t1"), rw("m"), Alu.mult)
            v.drain()
            v.tensor_tensor(rw("t1"), rw("t2"), rw("mi"), Alu.mult)
            v.tensor_tensor(lo(B), lo(A), rw("sm"), Alu.add)
            v.drain()
            v.tensor_tensor(hi(B), hi(A), rw("t1"), Alu.add)
            v.drain()
            v.tensor_tensor(rw("sm"), lo(B), hi(B), Alu.add)
            v.drain()
            P.inc(v.tensor_scalar(th(B), rw("sm"), 0.5, None, Alu.mult), "thr", 1)

    @blk.scalar
    def _(s):
        s.wait_ge(P.sems["qmm2"], 1)
        P.inc(s.activation(qm2f[0:2, :], pqm2[0:2, :], Act.Copy,
                           scale=1.0 / 2049.0), "qmm2", 1)

    @blk.vector
    def _(v):
        bqrow = P.sb["bqrow"]
        v.memset(biasr2[0:2, :], 0.0)
        v.drain()
        v.tensor_scalar(biasr2[0:1, :], bqrow[0:1, 0:512], 4096.0 / 2049.0,
                        None, Alu.mult)
        v.wait_ge(P.sems["qmm2"], 2)
        v.drain()
        P.inc(v.tensor_tensor(qm2[0:2, :], biasr2[0:2, :], qm2f[0:2, :],
                              Alu.add), "qmev", 2)

    LT = P.dram["LT"]

    @blk.sync
    def _(sy):
        sy.wait_ge(P.sems["esq"], 136)
        for m in range(17):
            if m >= 2:
                sy.wait_ge(P.sems["rnk"], m - 1)
            sy.dma_start(out=ltst[m % 2][:].rearrange("p (k f) -> p k f", f=128),
                         in_=LT[m]).then_inc(P.sems["ltld%d" % (m % 2)], 16)
            P.cnt["ltld%d" % (m % 2)] += 16

    @blk.tensor
    def _(te):
        te.wait_ge(P.sems["msk"], 1)
        for m in range(17):
            te.wait_ge(P.sems["ltld%d" % (m % 2)], 16 * (m // 2 + 1))
            if m >= 2:
                te.wait_ge(P.sems["rnkev"], m - 1)
            for k in range(m + 1):
                mm = te.matmul(prank[m % 2][:], ltst[m % 2][:, k * 128:(k + 1) * 128],
                               maskb[:, k * 8:(k + 1) * 8],
                               start=(k == 0), stop=(k == m))
            P.inc(mm, "rnk", 1)

    @blk.scalar
    def _(s):
        for m in range(17):
            s.wait_ge(P.sems["rnk"], m + 1)
            P.inc(s.activation(rank_sb[:, m * 8:(m + 1) * 8], prank[m % 2][:],
                               Act.Copy), "rnkev", 1)

    @blk.vector
    def _(v):
        v.wait_ge(P.sems["cld"], 48)
        v.wait_ge(P.sems["qmm2"], 1)
        for m in range(17):
            v.wait_ge(P.sems["rnkev"], m + 1)
            rb = bass.AP(rank_sb, m * 8, [[17 * 8, 128], [1, 8], [0, 64]])
            mb = bass.AP(maskb, m * 8, [[17 * 8, 128], [1, 8], [0, 64]])
            v.tensor_tensor(Ptmp[:].rearrange("p (h j) -> p h j", j=64), rb,
                            jp1[:].rearrange("p (h j) -> p h j", j=64), Alu.is_equal)
            v.drain()
            P.inc(v.tensor_tensor(P_sb[:, m * 512:(m + 1) * 512].rearrange(
                "p (h j) -> p h j", j=64), Ptmp[:].rearrange(
                "p (h j) -> p h j", j=64), mb, Alu.mult), "pbld", 1)

    @blk.tensor
    def _(te):
        te.wait_ge(P.sems["pbld"], 17)
        for g in range(4):
            for m in range(17):
                mm = te.matmul(pfs[:, g * 2:g * 2 + 2],
                               P_sb[:, m * 512 + g * 128: m * 512 + (g + 1) * 128],
                               fioAB[:, 2 * m:2 * m + 2],
                               start=(m == 0), stop=(m == 16))
            P.inc(mm, "fsl", 1)

    @blk.scalar
    def _(s):
        s.wait_ge(P.sems["fsl"], 4)
        P.inc(s.activation(ftmp[:], pfs[:], Act.Copy), "fsl", 1)

    @blk.vector
    def _(v):
        v.wait_ge(P.sems["fsl"], 5)
        fr = ftmp[:].rearrange("p (g c) -> p g c", c=2)
        P.inc(v.tensor_tensor(fsel[:], fr[:, :, 0:1], fr[:, :, 1:2], Alu.add),
              "fslv", 1)

    return dict(maskb=maskb, P_sb=P_sb, fsel=fsel, rank_sb=rank_sb, qm2=qm2)



def build_upto(phase="m3", debug=True):
    P = Prog()
    nc = P.nc
    P.din("xb", (T, D))
    P.din("xTb", (D, T))
    P.din("wqkv", (D, COLS))
    P.din("bqkv", (1, COLS))
    P.din("C2", (TH, 1152))
    P.din("S2n", (TH, 1152))
    P.din("twc", (128, 9))
    P.din("tws", (128, 9))
    P.din("twns", (128, 9))
    P.din("fioAB", (128, 34))
    P.din("jp1", (128, 512))
    P.din("LT", (FPAD, FPAD), BF16)
    P.din("qmw", (128, 100), BF16)
    with P.es:
        scA = P.es.enter_context(contextlib.ExitStack())
        with nc.Block() as blk:
            stA = emit_ln_proj(P, blk, scA)
            stB = emit_dft(P, blk, scA, stA["q_sb"])
            stC = emit_topk(P, blk, scA, stB["energy"], stA["q_sb"])
            P.sem("dump")

            @blk.gpsimd
            def _(g):
                if debug:
                    pairs = [("dQre", stB["Qre"][:], BF16),
                             ("dQim", stB["Qim"][:], BF16),
                             ("dener", stB["energy"][:], F32),
                             ("dmask", stC["maskb"][:], BF16),
                             ("dfsel", stC["fsel"][:], F32),
                             ("drank", stC["rank_sb"][:], F32)]
                    g.wait_ge(P.sems["fslv"], 1)
                    for name, t, dt in pairs:
                        g.dma_start(out=P.dram[name], in_=t).then_inc(P.sems["dump"], 16)
                        P.cnt["dump"] += 16
                    g.wait_ge(P.sems["dump"], P.n("dump"))
    return P


def make_debug_outs(P):
    P.dout("dQre", (128, 17 * 512), BF16)
    P.dout("dQim", (128, 17 * 512), BF16)
    P.dout("dener", (128, 17 * 8), F32)
    P.dout("dmask", (128, 17 * 8), BF16)
    P.dout("dfsel", (128, 4), F32)
    P.dout("drank", (128, 17 * 8), F32)


def emit_attn_tail(P, blk, scope, stA, stB, stC):
    nc = P.nc
    k_sb, v_sb = stA["k_sb"], stA["v_sb"]
    Qre, Qim, bqrow = stB["Qre"], stB["Qim"], stB["bqrow"]
    P_sb, fsel = stC["P_sb"], stC["fsel"]
    ones_col = P.sb["ones_col"]
    INV8 = 1.0 / (8.0 + 1e-8)

    ot_sb = P.at("ot_sb", (128, 4 * T), BF16, O_V)
    qm2 = P.sb["qm2"]
    krt = [P.sbuf(f"krt{i}", (128, 576), BF16, scope) for i in range(2)]
    selK = P.at("selK", (128, 4 * 576), BF16, O_Q + 17408)
    qselb = P.at("qselb", (128, 4 * 128), BF16, O_Q + 22016)
    qmrow = P.sbuf("qmrow", (1, 1024), F32, scope)
    cosE = P.at("cosE", (128, 2048), BF16, O_SP)
    cosO = P.at("cosO", (128, 2048), BF16, O_SP + 4096)
    cselTe = P.at("cselTe", (128, 2048), BF16, O_SP + 8192)
    cselTo = P.at("cselTo", (128, 2048), BF16, O_SP + 12288)
    kvselb = P.at("kvselb", (128, 4 * 256), BF16, O_Q + 23040)
    bkvt = P.sbuf("bkvt", (128, 256), BF16, scope)
    b4krow = P.sbuf("b4krow", (1, 1024), BF16, scope)
    prodb = P.sbuf("prodb", (128, 64), F32, scope)
    scball = P.sbuf("scball", (128, 4), F32, scope)
    m0all = P.sbuf("m0all", (128, 4), F32, scope)
    wlall = P.sbuf("wlall", (128, 4), F32, scope)
    nwlall = P.sbuf("nwlall", (128, 4), F32, scope)
    t4a = P.sbuf("t4a", (128, 4), F32, scope)
    sc4 = P.sbuf("sc4", (4, 128), F32, scope)
    e4 = P.sbuf("e4", (4, 128), F32, scope)
    e4x = P.sbuf("e4x", (4, 128), F32, scope)
    mx4 = P.sbuf("mx4", (4, 2), F32, scope)
    sm4 = P.sbuf("sm4", (4, 2), F32, scope)
    ln4 = P.sbuf("ln4", (4, 2), F32, scope)
    rc4 = P.sbuf("rc4", (4, 2), F32, scope)
    at4 = P.sbuf("at4", (4, 128), F32, scope)
    attcol = P.sbuf("attcol", (128, 4), F32, scope)
    ct1 = P.sbuf("ct1", (128, 128), BF16, scope)
    cc1 = P.sbuf("cc1", (128, 128), BF16, scope)
    cc2 = P.sbuf("cc2", (128, 128), BF16, scope)
    cosP = P.at("cosP", (128, T), BF16, O_K)
    sinP = P.at("sinP", (128, T), BF16, O_K + 8192)
    cva_t = P.sbuf("cva_t", (2, T), BF16, scope)
    cvb_t = P.sbuf("cvb_t", (2, T), BF16, scope)
    identb = P.sbuf("identb", (128, 128), BF16, scope)
    identf = P.sbuf("identf_t", (128, 128), F32, scope)

    for s in ["dld", "dkr0", "dkr1", "qsl", "qslev",
              "sel", "selev", "pl", "ptrx", "ptev", "kvm", "kvev", "bkv",
              "m0w", "sc", "sct", "sfm", "attb", "atev", "qmb", "cof",
              "irf", "otev"]:
        P.sem(s)

    SK = dict(A1c=0, A1s=64, B1c=128, B1s=192, A2c=256, A2s=288,
              B2ec=320, B2es=384, B2oc=448, B2os=512)

    def skA(g, name, t1len, t0len):
        off = g * 576 + SK[name]
        return bass.AP(selK, off, [[4 * 576, 128], [1, t1len], [0, t0len]])

    def skB(g, name, t1len, t0len):
        off = g * 576 + SK[name]
        return bass.AP(selK, off, [[4 * 576, 128], [0, t1len], [1, t0len]])

    @blk.gpsimd
    def _(g):
        g.wait_ge(P.sems["fslv"], 1)
        g.dma_start(out=cva_t[0:2, :], in_=P.dram["cva"]).then_inc(P.sems["dld"], 16)
        g.dma_start(out=cvb_t[0:2, :], in_=P.dram["cvb"]).then_inc(P.sems["dld"], 16)
        g.dma_start(out=identb[:], in_=P.dram["ident"]).then_inc(P.sems["dld"], 16)
        g.dma_start(out=identf[:], in_=P.dram["identf"]).then_inc(P.sems["dld"], 16)
        P.cnt["dld"] += 64
        for m in range(17):
            sname = "dkr%d" % (m % 2)
            if m >= 2:
                g.wait_ge(P.sems["sel"], 8 * (m - 1))
            g.dma_start(out=krt[m % 2][:], in_=P.dram["KR"][m]).then_inc(P.sems[sname], 16)
            P.cnt[sname] += 16
        pass

    with contextlib.ExitStack() as d0:
        pqs = P.psum("pqs", (128, 128), F32, d0)

        @blk.tensor
        def _(te):
            te.wait_ge(P.sems["fslv"], 1)
            for g in range(4):
                if g > 0:
                    te.wait_ge(P.sems["qslev"], g)
                for m in range(17):
                    mm = te.matmul(pqs[:],
                                   P_sb[:, m * 512 + g * 128:m * 512 + (g + 1) * 128],
                                   Qre[:, m * 512 + g * 128:m * 512 + (g + 1) * 128],
                                   start=(m == 0), stop=(m == 16))
                P.inc(mm, "qsl", 1)

        @blk.scalar
        def _(s):
            for g in range(4):
                s.wait_ge(P.sems["qsl"], g + 1)
                P.inc(s.activation(qselb[:, g * 128:(g + 1) * 128], pqs[:], Act.Copy),
                      "qslev", 1)
            P.inc(s.activation(b4krow[0:1, :], bqrow[0:1, 512:1536], Act.Copy,
                               scale=4096.0), "bkv", 1)

    with contextlib.ExitStack() as d1:
        psel = [P.psum(f"psel{i}", (128, 288), F32, d1) for i in range(8)]

        @blk.tensor
        def _(te):
            te.wait_ge(P.sems["qslev"], 4)
            te.wait_ge(P.sems["qmev"], 2)
            for m in range(17):
                te.wait_ge(P.sems["dkr%d" % (m % 2)], 16 * (m // 2 + 1))
                for g in range(4):
                    for half in range(2):
                        mm = te.matmul(
                            psel[g * 2 + half][:],
                            P_sb[:, m * 512 + g * 128:m * 512 + (g + 1) * 128],
                            krt[m % 2][:, half * 288:(half + 1) * 288],
                            start=(m == 0), stop=(m == 16))
                        if m == 16:
                            P.inc(mm, "sel", 1)
                        elif g == 3 and half == 1:
                            P.inc(mm, "sel", 8)

        @blk.scalar
        def _(s):
            s.wait_ge(P.sems["sel"], 8 * 16 + 8)
            for g in range(4):
                for half in range(2):
                    P.inc(s.activation(
                        selK[:, g * 576 + half * 288:g * 576 + (half + 1) * 288],
                        psel[g * 2 + half][:], Act.Copy), "selev", 1)

    @blk.vector
    def _(v):
        v.wait_ge(P.sems["fslv"], 1)
        v.tensor_scalar(m0all[:], fsel[:], 0.0, None, Alu.is_equal)
        v.tensor_scalar(t4a[:], fsel[:], 2048.0, None, Alu.is_equal)
        v.drain()
        v.tensor_tensor(wlall[:], m0all[:], t4a[:], Alu.add)
        v.drain()
        v.tensor_scalar(nwlall[:], wlall[:], 1.0 / T, -2.0 / T, Alu.mult, Alu.add)
        v.drain()
        P.inc(v.tensor_scalar(wlall[:], nwlall[:], -1.0, None, Alu.mult), "m0w", 1)

    with contextlib.ExitStack() as d2:
        pls1 = P.at("pls1", (128, T), BF16, O_SP + 16384)
        pls2 = P.at("pls2", (128, T), BF16, O_SP + 24576)
        onesrow_b = P.sbuf("onesrow_b", (1, 128), BF16, d2)
        r0 = P.sbuf("r0", (2, 128), BF16, d2)
        r1 = P.sbuf("r1", (2, 128), BF16, d2)
        ptr = [P.psum(f"ptr{i}", (128, 512), BF16, d2) for i in range(2)]
        pkv = P.psum("pkv", (128, 256), F32, d2)
        pmix = P.psum("pmix", (128, 256), F32, d2)
        pqmb = pmix
        pbk = pmix
        pxs = P.psum("pxs", (128, 128), F32, d2)
        psc = pxs
        pat = pxs
        pot = [P.psum(f"pot{i}", (128, 512), F32, d2) for i in range(2)]

        @blk.vector
        def _(v):
            v.memset(onesrow_b[:], 1.0)
            v.memset(r0[0:2, :], 0.0)
            v.memset(r1[0:2, :], 1.0)
            v.drain()
            v.memset(r0[0:1, :], 1.0)
            P.inc(v.memset(r1[0:1, :], 0.0), "m0w", 1)

        for g in range(4):
            @blk.vector
            def _(v, g=g):
                v.wait_ge(P.sems["selev"], 8)
                if g > 0:
                    v.wait_ge(P.sems["ptrx"], 8 * g)
                cE = cosE[:].rearrange("p (a b) -> p a b", b=64)
                cO = cosO[:].rearrange("p (a b) -> p a b", b=64)
                p1 = pls1[:, 0:2048].rearrange("p (a b) -> p a b", b=64)
                p2 = pls2[:, 0:2048].rearrange("p (a b) -> p a b", b=64)
                v.tensor_tensor(p1, skA(g, "A2c", 32, 64), skB(g, "B2ec", 32, 64),
                                Alu.mult)
                v.tensor_tensor(p2, skA(g, "A2s", 32, 64), skB(g, "B2es", 32, 64),
                                Alu.mult)
                v.tensor_tensor(cE, p1, p2, Alu.subtract)
                v.tensor_tensor(p1, skA(g, "A2c", 32, 64), skB(g, "B2oc", 32, 64),
                                Alu.mult)
                v.tensor_tensor(p2, skA(g, "A2s", 32, 64), skB(g, "B2os", 32, 64),
                                Alu.mult)
                P.inc(v.tensor_tensor(cO, p1, p2, Alu.subtract), "pl", 1)

            @blk.tensor
            def _(te, g=g):
                te.wait_ge(P.sems["pl"], g + 1)
                nb = 0
                for half, srcp in enumerate([cosE, cosO]):
                    for c4 in range(4):
                        bi = (g * 8 + half * 4 + c4)
                        if bi >= 2:
                            te.wait_ge(P.sems["ptev"], bi - 1)
                        for c in range(4):
                            cc = c4 * 4 + c
                            mm = te.transpose(ptr[bi % 2][:, c * 128:(c + 1) * 128],
                                              srcp[:, cc * 128:(cc + 1) * 128],
                                              identb[:])
                        P.inc(mm, "ptrx", 1)

            @blk.scalar
            def _(s, g=g):
                for half in range(2):
                    dstp = cselTe if half == 0 else cselTo
                    for c4 in range(4):
                        bi = (g * 8 + half * 4 + c4)
                        s.wait_ge(P.sems["ptrx"], bi + 1)
                        P.inc(s.activation(dstp[:, c4 * 512:(c4 + 1) * 512],
                                           ptr[bi % 2][:], Act.Copy), "ptev", 1)

            @blk.tensor
            def _(te, g=g):
                te.wait_ge(P.sems["ptev"], 8 * (g + 1))
                te.wait_ge(P.sems["bkv"], 1 + 2 * g)
                te.wait_ge(P.sems["m0w"], 2)
                if g > 0:
                    te.wait_ge(P.sems["kvev"], g)
                te.matmul(pbk[:, 0:128], onesrow_b[0:1, :],
                          b4krow[0:1, g * 128:(g + 1) * 128], start=True, stop=True)
                mm = te.matmul(pbk[:, 128:256], onesrow_b[0:1, :],
                               b4krow[0:1, 512 + g * 128:512 + (g + 1) * 128],
                               start=True, stop=True)
                P.inc(mm, "bkv", 1)
                for part, src in [(1, v_sb), (0, k_sb)]:
                    for m in range(32):
                        cs = cselTe if m < 16 else cselTo
                        mc = m % 16
                        mm = te.matmul(pkv[:, part * 128:(part + 1) * 128],
                                       cs[:, mc * 128:(mc + 1) * 128],
                                       src[:, m * 512 + g * 128:m * 512 + (g + 1) * 128],
                                       start=(m == 0), stop=(m == 31))
                P.inc(mm, "kvm", 1)

            @blk.scalar
            def _(s, g=g):
                s.wait_ge(P.sems["bkv"], 2 + 2 * g)
                P.inc(s.activation(bkvt[:], pbk[:], Act.Copy), "bkv", 1)

            @blk.vector
            def _(v, g=g):
                v.wait_ge(P.sems["kvm"], g + 1)
                v.wait_ge(P.sems["bkv"], 3 + 2 * g)
                kv = kvselb[:, g * 256:(g + 1) * 256]
                P.inc(v.scalar_tensor_tensor(kv, bkvt[:], m0all[:, g:g + 1],
                                             pkv[:], Alu.mult, Alu.add), "kvev", 1)
                v.drain()
                v.tensor_tensor(prodb[0:64, :], qselb[0:64, g * 128:g * 128 + 64],
                                kvselb[0:64, g * 256:g * 256 + 64], Alu.mult)
                v.tensor_tensor(prodb[64:128, :],
                                qselb[64:128, g * 128 + 64:g * 128 + 128],
                                kvselb[64:128, g * 256 + 64:g * 256 + 128], Alu.mult)
                v.drain()
                P.inc(v.tensor_reduce(scball[:, g:g + 1], prodb[:], axis=AX.X,
                                      op=Alu.add), "sc", 1)

        @blk.tensor
        def _(te):
            te.wait_ge(P.sems["sc"], 4)
            P.inc(te.transpose(psc[0:4, 0:128], scball[:], identf[:]), "sct", 1)

        @blk.vector
        def _(v):
            v.wait_ge(P.sems["sct"], 1)
            v.tensor_copy(sc4[:], psc[0:4, 0:128])
            v.drain()
            P.inc(v.tensor_reduce(mx4[:], sc4[:].rearrange("p (h j) -> p h j", j=64),
                                  axis=AX.X, op=Alu.max), "sfm", 1)
            v.drain()
            mxb = bass.AP(mx4, 0, [[2, 4], [1, 2], [0, 64]])
            P.inc(v.tensor_tensor(e4[:].rearrange("p (h j) -> p h j", j=64),
                                  sc4[:].rearrange("p (h j) -> p h j", j=64),
                                  mxb, Alu.subtract), "sfm", 1)

        @blk.scalar
        def _(s):
            s.wait_ge(P.sems["sfm"], 2)
            P.inc(s.activation(e4x[:], e4[:], Act.Exp, scale=INV8), "sfm", 1)

        @blk.vector
        def _(v):
            v.wait_ge(P.sems["sfm"], 3)
            P.inc(v.tensor_reduce(sm4[:], e4x[:].rearrange("p (h j) -> p h j", j=64),
                                  axis=AX.X, op=Alu.add), "sfm", 1)

        @blk.scalar
        def _(s):
            s.wait_ge(P.sems["sfm"], 4)
            s.activation(ln4[:], sm4[:], Act.Ln)
            s.drain()
            P.inc(s.activation(rc4[:], ln4[:], Act.Exp, scale=-1.0), "sfm", 1)

        @blk.vector
        def _(v):
            v.wait_ge(P.sems["sfm"], 5)
            rcb = bass.AP(rc4, 0, [[2, 4], [1, 2], [0, 64]])
            P.inc(v.tensor_tensor(at4[:].rearrange("p (h j) -> p h j", j=64),
                                  e4x[:].rearrange("p (h j) -> p h j", j=64),
                                  rcb, Alu.mult), "attb", 1)

        @blk.tensor
        def _(te):
            te.wait_ge(P.sems["attb"], 1)
            P.inc(te.transpose(pat[:, 0:4], at4[0:4, :], identf[0:4, 0:4]), "attb", 1)

        @blk.vector
        def _(v):
            v.wait_ge(P.sems["attb"], 2)
            P.inc(v.tensor_copy(attcol[:], pat[:, 0:4]), "atev", 1)

        for g in range(4):
            @blk.tensor
            def _(te, g=g):
                te.wait_ge(P.sems["qmev"], 2)
                te.wait_ge(P.sems["atev"], 1)
                if g > 0:
                    te.wait_ge(P.sems["cof"], 2 * g - 1)
                te.matmul(pqmb[:, 0:128], r0[0:2, :],
                          qm2[0:2, g * 128:(g + 1) * 128], start=True, stop=True)
                mm = te.matmul(pqmb[:, 128:256], r1[0:2, :],
                               qm2[0:2, g * 128:(g + 1) * 128], start=True, stop=True)
                P.inc(mm, "qmb", 1)

            @blk.vector
            def _(v, g=g):
                v.wait_ge(P.sems["qmb"], g + 1)
                v.tensor_scalar(ct1[:], kvselb[:, g * 256 + 128:g * 256 + 256],
                                attcol[:, g:g + 1], wlall[:, g:g + 1],
                                Alu.mult, Alu.mult)
                v.drain()
                v.scalar_tensor_tensor(cc1[:], pqmb[:, 0:128], nwlall[:, g:g + 1],
                                       ct1[:], Alu.mult, Alu.add)
                P.inc(v.tensor_scalar(cc2[:], pqmb[:, 128:256], wlall[:, g:g + 1],
                                      None, Alu.mult), "cof", 1)
                if g > 0:
                    v.wait_ge(P.sems["irf"], 8 * g)
                c1 = pls1[:].rearrange("p (a b) -> p a b", b=64)
                c2 = pls2[:].rearrange("p (a b) -> p a b", b=64)
                v.tensor_tensor(c1, skA(g, "A1c", 64, 64), skB(g, "B1c", 64, 64),
                                Alu.mult)
                v.tensor_tensor(c2, skA(g, "A1s", 64, 64), skB(g, "B1s", 64, 64),
                                Alu.mult)
                v.tensor_tensor(cosP[:].rearrange("p (a b) -> p a b", b=64), c1, c2,
                                Alu.subtract)
                v.tensor_tensor(c1, skA(g, "A1s", 64, 64), skB(g, "B1c", 64, 64),
                                Alu.mult)
                v.tensor_tensor(c2, skA(g, "A1c", 64, 64), skB(g, "B1s", 64, 64),
                                Alu.mult)
                P.inc(v.tensor_tensor(sinP[:].rearrange("p (a b) -> p a b", b=64),
                                      c1, c2, Alu.add), "cof", 1)

            @blk.tensor
            def _(te, g=g):
                te.wait_ge(P.sems["cof"], 2 * (g + 1))
                te.wait_ge(P.sems["qmev"], 2)
                for ch in range(8):
                    bi = g * 8 + ch
                    if bi >= 2:
                        te.wait_ge(P.sems["otev"], bi - 1)
                    bank = pot[bi % 2]
                    sl = slice(ch * 512, (ch + 1) * 512)
                    te.matmul(bank[:], qm2[0:2, g * 128:(g + 1) * 128],
                              cva_t[0:2, sl], start=True, stop=False)
                    te.matmul(bank[0:64, :], cc1[0:64, 0:64], cosP[0:64, sl],
                              start=False, stop=False, tile_position=(0, 0),
                              skip_group_check=True)
                    te.matmul(bank[64:128, :], cc1[64:128, 64:128], cosP[64:128, sl],
                              start=False, stop=False, tile_position=(64, 64),
                              skip_group_check=True)
                    te.matmul(bank[0:64, :], cc2[0:64, 0:64], sinP[0:64, sl],
                              start=False, stop=False, tile_position=(0, 0),
                              skip_group_check=True)
                    te.matmul(bank[64:128, :], cc2[64:128, 64:128],
                              sinP[64:128, sl],
                              start=False, stop=False, tile_position=(64, 64),
                              skip_group_check=True)
                    mm = te.matmul(bank[:], qm2[0:2, g * 128:(g + 1) * 128],
                                   cvb_t[0:2, sl], start=False, stop=True)
                    P.inc(mm, "irf", 1)

            @blk.scalar
            def _(s, g=g):
                for ch in range(8):
                    bi = g * 8 + ch
                    s.wait_ge(P.sems["irf"], bi + 1)
                    P.inc(s.activation(ot_sb[:, g * T + ch * 512:g * T + (ch + 1) * 512],
                                       pot[bi % 2][:], Act.Copy), "otev", 1)

    return dict(ot_sb=ot_sb, qm2=qm2)

EO_SCR = 57344
EO_W1 = 73728
EO_W2 = 106496
EO_HB = 139264
EO_H2 = 174368
EO_XT = 190752
EO_XA = 199008
MAGIC = 12582912.0
LN127 = float(np.log(127.0))
NCH = 8

N_CCS, N_WOM, N_WOA, N_XEV, N_Q1 = 128, 16, 2, 10, 30
N_TRX, N_TREV, N_RR, N_M1, N_HLO = 9, 9, 8, 16, 13
N_HBA, N_CNV, N_S2, N_QS, N_M2, N_CTV = 16, 48, 8, 48, 4, 4
N_WOP = 256


def emit_ef(P, blk, scope, ot_sb, out_ap, dbg_fn=None):
    nc = P.nc
    w1q = P.at("w1q", (128, 8 * 2048), BF16, EO_W1)
    w2q = P.at("w2q", (128, 16 * 1024), BF16, EO_W2)
    hb = P.at("hb", (128, 17 * 516), F32, EO_HB)
    h2q = P.at("h2q", (128, 16 * 512), BF16, EO_H2)
    xq1T = P.at("xq1T", (128, 8 * 516), BF16, EO_XT)
    xa = P.at("xa", (128, 4 * 1024), F32, EO_XA)
    xn2f = P.at("xn2f", (128, 1024), F32, EO_SCR)
    xq1b = [P.at(f"xq1b{i}", (128, 1024), F32, EO_SCR + 4096 + 4096 * i)
            for i in range(2)]
    xmini = P.at("xmini", (2, 1024), F32, EO_SCR + 12288)
    rbig = P.at("rbig", (128, 516), F32, EO_SCR)
    s2big = P.at("s2big", (128, 512), F32, EO_SCR)
    cbufA = P.at("cbufA", (128, 1024), F32, EO_SCR + 2080)
    cbufB = P.at("cbufB", (128, 1024), F32, EO_SCR + 6176)
    acc2 = [P.at(f"acc2{i}", (128, 512), F32, EO_SCR + 10272 + 2048 * i)
            for i in range(2)]
    wbuf = [P.at(f"wbuf{i}", (128, 2048), F32, EO_HB + 8192 * i)
            for i in range(2)]
    wsq = [P.at(f"wsqp{i}", (128, 2048), F32, EO_HB + 16384 + 8192 * i)
           for i in range(2)]
    wtr = wsq[1]
    wmk = P.at("wmk", (128, 2048), F32, EO_H2)
    wsn = P.at("wsn", (128, 2048), F32, EO_H2 + 8192)
    wob_sb = P.at("wob_sb", (128, 8 * 1024), BF16, EO_W1)

    cc_in = nc.dram_tensor("cc_in", [4, 128, T], BF16)
    cc_out = nc.dram_tensor("cc_out", [8, 128, T], BF16)
    wob = nc.dram_tensor("wob", [1024, 1024], BF16)
    amx_i = nc.dram_tensor("amx_i", [128, 4], F32)
    amx_o = nc.dram_tensor("amx_o", [128, 4], F32)
    ws_i = nc.dram_tensor("ws_i", [1, 128], F32)
    ws_o = nc.dram_tensor("ws_o", [1, 128], F32)
    rs_in = nc.dram_tensor("rs_in", [T, D], F32)
    rs_out = nc.dram_tensor("rs_out", [TH, D], F32)

    borow = P.sbuf("borow", (1, 1024), BF16, scope)
    xamini = P.sbuf("xamini", (2, 1024), F32, scope)
    ccsb = P.sbuf("ccsb", (128, 8 * 520), BF16, scope)
    xtile = [P.sbuf(f"xtile{i}", (128, 512), F32, scope) for i in range(2)]
    wop = [P.sbuf(f"wop{i}", (128, 512), BF16, scope) for i in range(2)]
    identb2 = P.sbuf("identb2", (128, 128), BF16, scope)
    identf2 = P.sbuf("identf2", (128, 128), F32, scope)
    onesr_b = P.sbuf("onesr_b", (1, 128), BF16, scope)
    onesr_f = P.sbuf("onesr_f", (1, 128), F32, scope)
    onesc_f = P.sbuf("onesc_f", (128, 1), F32, scope)
    tcol = [P.sbuf(f"tcolw{i}", (128, 1), F32, scope) for i in range(2)]
    wsums = P.sbuf("wsums", (1, 128), F32, scope)
    wsg = P.sbuf("wsg", (1, 128), F32, scope)
    mwraw = P.sbuf("mwraw", (1, 2), F32, scope)
    mwc = P.sbuf("mwc", (1, 2), F32, scope)
    mw127 = P.sbuf("mw127", (1, 2), F32, scope)
    swr = P.sbuf("swr", (1, 2), F32, scope)
    swt1 = P.sbuf("swt1", (1, 2), F32, scope)
    swt2 = P.sbuf("swt2", (1, 2), F32, scope)
    swf = P.sbuf("swf", (1, 2), F32, scope)
    swbc = P.sbuf("swbc", (128, 2), F32, scope)
    bc2 = P.sbuf("bc2", (128, 1), F32, scope)
    cwsb = P.sbuf("cwsb", (128, 48), F32, scope)
    cbsb = P.sbuf("cbsb", (128, 16), F32, scope)
    st6 = P.sbuf("st6", (128, 12), F32, scope)
    mv4 = P.sbuf("mv4", (128, 4), F32, scope)
    u0 = P.sbuf("u0w", (128, 1), F32, scope)
    u1 = P.sbuf("u1w", (128, 1), F32, scope)
    msm = P.sbuf("msm", (128, 1), F32, scope)
    u2 = P.sbuf("u2w", (128, 1), F32, scope)
    u3 = P.sbuf("u3w", (128, 1), F32, scope)
    negm = P.sbuf("negm", (128, 1), F32, scope)
    tvar = P.sbuf("tvar", (128, 1), F32, scope)
    lvar = P.sbuf("lvar", (128, 1), F32, scope)
    rstd2 = P.sbuf("rstd2", (128, 1), F32, scope)
    amax1 = P.sbuf("amax1", (128, 1), F32, scope)
    la1 = P.sbuf("la1", (128, 1), F32, scope)
    sfac = P.sbuf("sfac", (128, 1), F32, scope)
    amx1 = P.sbuf("amx1", (128, 4), F32, scope)
    amx1m = P.sbuf("amx1m", (2, 1), F32, scope)
    rrow = P.sbuf("rrow", (1, 516), F32, scope)
    hE = P.sbuf("hE", (2, 512), F32, scope)
    ptrh = P.sbuf("ptrh", (128, 32), F32, scope)
    absb = P.sbuf("absb", (128, 512), F32, scope)
    amx_sb = P.sbuf("amx_sb", (128, 4), F32, scope)
    amx2_sb = P.sbuf("amx2_sb", (128, 4), F32, scope)
    amax2c = P.sbuf("amax2c", (128, 4), F32, scope)
    r2sb = P.sbuf("r2sb", (128, 4), F32, scope)
    s2l = P.sbuf("s2l", (128, 4), F32, scope)
    s2sb = P.sbuf("s2sb", (128, 4), F32, scope)
    s2row = P.sbuf("s2row", (1, 512), F32, scope)
    cL127 = P.sbuf("cL127", (128, 1), F32, scope)
    cMAG = P.sbuf("cMAG", (128, 1), F32, scope)
    cNMAG = P.sbuf("cNMAG", (128, 1), F32, scope)
    nc.const_aps.register(F32, LN127, cL127[:, 0:1])
    nc.const_aps.register(F32, MAGIC, cMAG[:, 0:1])
    nc.const_aps.register(F32, -MAGIC, cNMAG[:, 0:1])

    pA = P.psum("pA", (128, 512), F32, scope)
    pB = P.psum("pB", (128, 512), F32, scope)
    pC = P.psum("pC", (128, 512), F32, scope)
    pD = P.psum("pD", (128, 512), F32, scope)
    pE = P.psum("pE", (2, 512), F32, scope)
    ptb = P.psum("ptb", (128, 512), F32, scope)
    ptf = P.psum("ptf", (128, 512), F32, scope)
    pR = P.psum("pR", (1, 512), F32, scope)
    WOB = [pA, pB, pC, pD]

    for s in ["efw", "ccd", "ccg", "f0d0", "f0d1", "f0a", "f0v", "f0t",
              "wsm", "wsd", "wqd0", "wqd1", "wqa", "wqv", "wqs",
              "ccs", "xl0", "xl1", "wopd0", "wopd1", "wom", "woa", "xev",
              "q1", "trx", "trev", "rr", "m1", "hlo", "hba", "cnv", "s2",
              "amd", "arm", "qs", "m2", "ctv", "cst", "rsd"]:
        P.sem(s)

    tiles = [("w1t", k * 128, 2048, 0, k) for k in range(8)] + \
            [("w2t", j * 128, 1024, 1, j) for j in range(16)]
    HBS = 17 * 516

    def slabv(k, c0, c1):
        return ccsb[:, k * 520 + c0:k * 520 + c1]

    def hslot(s, c0, c1):
        return hb[:, s * 516 + c0:s * 516 + c1]

    def xts(k, c0, c1):
        return xq1T[:, k * 516 + c0:k * 516 + c1]

    @blk.sync
    def _(sy):
        sy.wait_ge(P.sems["otev"], 32)
        sy.dma_start(out=cwsb[:], in_=P.dram["cwh"]).then_inc(P.sems["efw"], 16)
        sy.dma_start(out=cbsb[:], in_=P.dram["cbh"]).then_inc(P.sems["efw"], 16)
        sy.dma_start(out=identb2[:], in_=P.dram["ident"]).then_inc(P.sems["efw"], 16)
        sy.dma_start(out=identf2[:], in_=P.dram["identf"]).then_inc(P.sems["efw"], 16)
        P.cnt["efw"] += 16 * 4

    @blk.gpsimd
    def _(g):
        g.wait_ge(P.sems["otev"], 32)
        for k in range(8):
            g.dma_start(out=wob_sb[:, k * 1024:(k + 1) * 1024],
                        in_=P.dram["wo"][k * 128:(k + 1) * 128, :])\
             .then_inc(P.sems["ccd"], 16)
        g.dma_start(out=borow[:], in_=P.dram["bo"]).then_inc(P.sems["ccd"], 16)
        g.wait_ge(P.sems["ccd"], 16 * 9)
        for k in range(8):
            g.dma_start(out=wob.ap()[k * 128:(k + 1) * 128, :],
                        in_=wob_sb[:, k * 1024:(k + 1) * 1024])\
             .then_inc(P.sems["ccd"], 16)
        for gg in range(4):
            g.dma_start(out=cc_in.ap()[gg], in_=ot_sb[:, gg * T:(gg + 1) * T])\
             .then_inc(P.sems["ccd"], 16)
        P.cnt["ccd"] += 16 * 21
        g.wait_ge(P.sems["ccd"], 16 * 21)
        g.collective_compute(
            "AllGather", Alu.bypass,
            replica_groups=[[0, 1], [2, 3], [4, 5], [6, 7]],
            ins=[cc_in.ap().opt()], outs=[cc_out.ap().opt()],
        ).then_inc(P.sems["ccg"], 1)
        P.cnt["ccg"] += 1
        for i, (nm, r0, ncol, knd, sl) in enumerate(tiles):
            if i >= 2:
                g.wait_ge(P.sems["f0a"], i - 1)
            g.dma_start(out=wbuf[i % 2][:, 0:ncol],
                        in_=P.dram[nm][r0:r0 + 128, :])\
             .then_inc(P.sems["f0d%d" % (i % 2)], 16)
            P.cnt["f0d%d" % (i % 2)] += 16
        g.wait_ge(P.sems["wsm"], 2)
        g.dma_start(out=ws_i.ap(), in_=wsums[:]).then_inc(P.sems["wsd"], 16)
        g.wait_ge(P.sems["wsd"], 16)
        g.collective_compute(
            "AllReduce", Alu.add,
            replica_groups=[[0, 1], [2, 3], [4, 5], [6, 7]],
            ins=[ws_i.ap().opt()], outs=[ws_o.ap().opt()],
        ).then_inc(P.sems["ccg"], 1)
        P.cnt["ccg"] += 1
        g.wait_ge(P.sems["ccg"], 2)
        g.dma_start(out=wsg[:], in_=ws_o.ap()).then_inc(P.sems["wsd"], 16)
        P.cnt["wsd"] += 32
        for i, (nm, r0, ncol, knd, sl) in enumerate(tiles):
            if i >= 2:
                g.wait_ge(P.sems["wqa"], i - 1)
            g.dma_start(out=wbuf[i % 2][:, 0:ncol],
                        in_=P.dram[nm][r0:r0 + 128, :])\
             .then_inc(P.sems["wqd%d" % (i % 2)], 16)
            P.cnt["wqd%d" % (i % 2)] += 16

    @blk.scalar
    def _(s):
        for i, (nm, r0, ncol, knd, sl) in enumerate(tiles):
            s.wait_ge(P.sems["f0d%d" % (i % 2)], 16 * (i // 2 + 1))
            if i >= 2:
                s.wait_ge(P.sems["f0v"], i - 1)
            P.inc(s.activation(wsq[i % 2][:, 0:ncol], wbuf[i % 2][:, 0:ncol],
                               Act.Abs), "f0a", 1)

    @blk.vector
    def _(v):
        v.wait_ge(P.sems["otev"], 32)
        v.memset(onesr_b[:], 1.0)
        v.memset(onesr_f[:], 1.0)
        v.memset(onesc_f[:], 1.0)
        v.memset(cL127[:], LN127)
        v.memset(cMAG[:], MAGIC)
        v.memset(cNMAG[:], -MAGIC)
        v.memset(wsums[:], 0.0)
        v.drain()
        for i, (nm, r0, ncol, knd, sl) in enumerate(tiles):
            v.wait_ge(P.sems["f0a"], i + 1)
            if i >= 2:
                v.wait_ge(P.sems["f0t"], i - 1)
            P.inc(v.tensor_reduce(tcol[i % 2][:], wsq[i % 2][:, 0:ncol],
                                  axis=AX.X, op=Alu.add), "f0v", 1)

    @blk.tensor
    def _(te):
        te.wait_ge(P.sems["otev"], 32)
        for i, (nm, r0, ncol, knd, sl) in enumerate(tiles):
            te.wait_ge(P.sems["f0v"], i + 1)
            mm = te.matmul(pR[0:1, knd:knd + 1], tcol[i % 2][:], onesc_f[:],
                           start=(i == 0 or i == 8), stop=(i == 7 or i == 23))
            P.inc(mm, "f0t", 1)

    @blk.scalar
    def _(s):
        s.wait_ge(P.sems["f0t"], 24)
        P.inc(s.activation(wsums[0:1, 0:2], pR[0:1, 0:2], Act.Copy), "wsm", 2)
        s.wait_ge(P.sems["wsm"], 3)
        s.activation(swt1[0:1, :], mwc[0:1, :], Act.Ln)
        s.drain()
        P.inc(s.activation(swr[0:1, :], swt1[0:1, :], Act.Exp, scale=-1.0),
              "wsm", 1)

    @blk.vector
    def _(v):
        v.wait_ge(P.sems["wsd"], 32)
        v.tensor_scalar(mwraw[0:1, :], wsg[0:1, 0:2], 1.0 / (4096.0 * 1024.0),
                        None, Alu.mult)
        v.drain()
        P.inc(v.tensor_scalar(mwc[0:1, :], mwraw[0:1, :], 1e-5, None, Alu.max),
              "wsm", 1)
        v.wait_ge(P.sems["wsm"], 4)
        v.tensor_tensor(swt1[0:1, :], mwc[0:1, :], swr[0:1, :], Alu.mult)
        v.drain()
        v.tensor_scalar(swt2[0:1, :], swt1[0:1, :], -1.0, 2.0, Alu.mult, Alu.add)
        v.drain()
        v.tensor_tensor(swf[0:1, :], swr[0:1, :], swt2[0:1, :], Alu.mult)
        v.drain()
        P.inc(v.tensor_scalar(mw127[0:1, :], mwc[0:1, :], 1.0 / 127.0, None,
                              Alu.mult), "wsm", 1)

    @blk.tensor
    def _(te):
        te.wait_ge(P.sems["wsm"], 5)
        te.matmul(ptf[:, 0:1], onesr_f[:], swf[0:1, 0:1], start=True, stop=True)
        te.matmul(ptf[:, 1:2], onesr_f[:], swf[0:1, 1:2], start=True, stop=True)
        P.inc(te.matmul(ptf[:, 2:3], onesr_f[:], mw127[0:1, 1:2],
                        start=True, stop=True), "wsm", 1)

    @blk.vector
    def _(v):
        v.wait_ge(P.sems["wsm"], 6)
        v.tensor_copy(swbc[:], ptf[:, 0:2])
        v.drain()
        P.inc(v.tensor_copy(bc2[:], ptf[:, 2:3]), "wsm", 1)

    @blk.scalar
    def _(s):
        s.wait_ge(P.sems["wsm"], 7)
        for i, (nm, r0, ncol, knd, sl) in enumerate(tiles):
            s.wait_ge(P.sems["wqd%d" % (i % 2)], 16 * (i // 2 + 1))
            if i >= 1:
                s.wait_ge(P.sems["wqv"], i)
            s.activation(wsq[0][:, 0:ncol], wbuf[i % 2][:, 0:ncol], Act.Square,
                         scale=swbc[:, knd:knd + 1])
            P.inc(s.activation(wsn[:, 0:ncol], wbuf[i % 2][:, 0:ncol],
                               Act.Sign), "wqa", 1)
            if i >= 1:
                nm2, r02, nc2, knd2, sl2 = tiles[i - 1]
                dst = w1q[:, sl2 * 2048:(sl2 + 1) * 2048] if knd2 == 0 else \
                    w2q[:, sl2 * 1024:(sl2 + 1) * 1024]
                s.wait_ge(P.sems["wqv"], i)
                P.inc(s.activation(dst, wtr[:, 0:nc2], Act.Copy), "wqs", 1)
        s.wait_ge(P.sems["wqv"], 24)
        P.inc(s.activation(w2q[:, 15 * 1024:16 * 1024], wtr[:, 0:1024],
                           Act.Copy), "wqs", 1)

    @blk.vector
    def _(v):
        for i, (nm, r0, ncol, knd, sl) in enumerate(tiles):
            v.wait_ge(P.sems["wqa"], i + 1)
            if i >= 1:
                v.wait_ge(P.sems["wqs"], i)
            v.tensor_scalar(wmk[:, 0:ncol], wsq[0][:, 0:ncol], 0.25, None,
                            Alu.is_gt)
            v.drain()
            P.inc(v.tensor_tensor(wtr[:, 0:ncol], wmk[:, 0:ncol],
                                  wsn[:, 0:ncol], Alu.mult), "wqv", 1)

    if dbg_fn is not None:
        dbg_fn()
    for c in range(NCH):
        t0 = c * 512
        lh = t0 - 1 if c > 0 else 0
        rh = t0 + 512 if c < NCH - 1 else T - 1
        srclo = min(max(t0 - 1, 0), T - 516)
        m_off = t0 - srclo
        lh_off = lh - srclo
        rh_off = rh - srclo

        @blk.sync
        def _(sy, c=c, t0=t0, lh=lh, rh=rh, srclo=srclo):
            sy.wait_ge(P.sems["ccg"], 1)
            if c > 0:
                sy.wait_ge(P.sems["wom"], N_WOM * c)
                sy.wait_ge(P.sems["hlo"], N_HLO * c)
            for k in range(8):
                sy.dma_start(out=slabv(k, 0, 516),
                             in_=cc_out.ap()[k][:, srclo:srclo + 516])\
                  .then_inc(P.sems["ccs"], 16)
            P.cnt["ccs"] += N_CCS
            base_wom = N_WOM * c
            if c == 0:
                sy.wait_ge(P.sems["ccd"], 16 * 21)
            if c > 0:
                sy.wait_ge(P.sems["ctv"], N_CTV * c)
            sy.dma_start(out=xmini[0:2, :],
                         in_=P.dram["xb"][lh:rh + 1:rh - lh, :])\
              .then_inc(P.sems["xl0"], 16)
            P.cnt["xl0"] += 16

            def panel(nh, k):
                i = nh * 8 + k
                if i >= 2:
                    sy.wait_ge(P.sems["wom"], base_wom + i - 1)
                sy.dma_start(
                    out=wop[i % 2][:],
                    in_=wob.ap()[k * 128:(k + 1) * 128,
                                 nh * 512:(nh + 1) * 512])\
                  .then_inc(P.sems["wopd%d" % (i % 2)], 16)
                P.cnt["wopd%d" % (i % 2)] += 16

            def xhalf(nh, s):
                jj = nh * 5 + s
                if jj >= 2:
                    sy.wait_ge(P.sems["xev"], N_XEV * c + jj - 1)
                sy.dma_start(
                    out=xtile[jj % 2][:],
                    in_=P.dram["xb"][t0 + s * 128:t0 + (s + 1) * 128,
                                     nh * 512:(nh + 1) * 512])\
                  .then_inc(P.sems["xl%d" % (jj % 2)], 16)
                P.cnt["xl%d" % (jj % 2)] += 16

            for k in range(8):
                panel(0, k)
            for s in range(4):
                xhalf(0, s)
            for k in range(8):
                panel(1, k)
            for s in range(4):
                xhalf(1, s)

        @blk.tensor
        def _(te, c=c, m_off=m_off, lh_off=lh_off, rh_off=rh_off):
            te.wait_ge(P.sems["ccs"], N_CCS * (c + 1))
            if c > 0:
                te.wait_ge(P.sems["ctv"], N_CTV * c)
                te.wait_ge(P.sems["hlo"], N_HLO * c)
            base_wom = N_WOM * c
            for nh in range(2):
                if nh == 1:
                    te.wait_ge(P.sems["xev"], N_XEV * c + 5)
                for k in range(8):
                    i = nh * 8 + k
                    te.wait_ge(P.sems["wopd%d" % (i % 2)],
                               P.cnt["wopd%d" % (i % 2)]
                               - (128 if i % 2 == 0 else 128)
                               + 16 * (i // 2 + 1))
                    for s in range(4):
                        te.matmul(WOB[s][:], slabv(k, m_off + s * 128,
                                                   m_off + (s + 1) * 128),
                                  wop[i % 2][:], start=(k == 0), stop=False)
                    lhT = bass.AP(ccsb, k * 520 + lh_off,
                                  [[8 * 520, 128], [rh_off - lh_off, 2]])
                    mm = te.matmul(pE[:], lhT, wop[i % 2][:],
                                   start=(k == 0), stop=False)
                    P.inc(mm, "wom", 1)
                for s in range(4):
                    te.matmul(WOB[s][:], onesr_b[:],
                              borow[0:1, nh * 512:nh * 512 + 512],
                              start=False, stop=True)
                mm = te.matmul(pE[:], onesr_b[0:1, 0:2],
                               borow[0:1, nh * 512:nh * 512 + 512],
                               start=False, stop=True)
                P.inc(mm, "woa", 1)


        @blk.vector
        def _(v, c=c):
            ord0 = 80 * c + 16
            ord1 = 64 * c
            cnt_par = {0: ord0, 1: ord1}
            for nh in range(2):
                v.wait_ge(P.sems["woa"], N_WOA * c + nh + 1)
                for s in range(4):
                    jj = nh * 5 + s
                    cnt_par[jj % 2] += 16
                    v.wait_ge(P.sems["xl%d" % (jj % 2)], cnt_par[jj % 2])
                    ins = v.scalar_tensor_tensor(
                        xa[:, s * 1024 + nh * 512:s * 1024 + nh * 512 + 512],
                        WOB[s][:], 1.0, xtile[jj % 2][:],
                        Alu.mult, Alu.add)
                    P.inc(ins, "xev", 1)
                if nh == 0:
                    v.wait_ge(P.sems["xl0"], 80 * c + 16)
                P.inc(v.scalar_tensor_tensor(
                    xamini[0:2, nh * 512:nh * 512 + 512], pE[0:2, :], 1.0,
                    xmini[0:2, nh * 512:nh * 512 + 512], Alu.mult, Alu.add),
                    "xev", 1)

        for s in range(5):
            mini = (s == 4)
            np_ = 2 if mini else 128

            @blk.vector
            def _(v, c=c, s=s, mini=mini, np_=np_):
                if mini:
                    v.wait_ge(P.sems["xev"], N_XEV * (c + 1))
                    xlo = xamini[0:2, 0:512]
                    xhi = xamini[0:2, 512:1024]
                else:
                    v.wait_ge(P.sems["xev"], N_XEV * (c + 1))
                    xlo = xa[:, s * 1024:s * 1024 + 512]
                    xhi = xa[:, s * 1024 + 512:(s + 1) * 1024]
                if c > 0 and s == 0:
                    v.wait_ge(P.sems["cst"], 64 * c)
                v.bn_stats(st6[0:np_, 0:6], xlo)
                v.bn_stats(st6[0:np_, 6:12], xhi)
                v.drain()
                v.bn_aggr(mv4[0:np_, 0:2], st6[0:np_, 0:6])
                v.bn_aggr(mv4[0:np_, 2:4], st6[0:np_, 6:12])
                v.drain()
                v.tensor_tensor(u0[0:np_, :], mv4[0:np_, 0:1], mv4[0:np_, 2:3],
                                Alu.subtract)
                v.tensor_tensor(u1[0:np_, :], mv4[0:np_, 1:2], mv4[0:np_, 3:4],
                                Alu.add)
                v.tensor_tensor(msm[0:np_, :], mv4[0:np_, 0:1], mv4[0:np_, 2:3],
                                Alu.add)
                v.drain()
                v.tensor_tensor(u2[0:np_, :], u0[0:np_, :], u0[0:np_, :],
                                Alu.mult)
                v.tensor_scalar(u3[0:np_, :], u1[0:np_, :], 0.5, 1e-5,
                                Alu.mult, Alu.add)
                v.tensor_scalar(negm[0:np_, :], msm[0:np_, :], -0.5, None,
                                Alu.mult)
                v.drain()
                P.inc(v.scalar_tensor_tensor(tvar[0:np_, :], u2[0:np_, :],
                                             0.25, u3[0:np_, :], Alu.mult,
                                             Alu.add), "q1", 1)

            @blk.scalar
            def _(sc, c=c, s=s, np_=np_):
                base = N_Q1 * c + 6 * s
                sc.wait_ge(P.sems["q1"], base + 1)
                sc.activation(lvar[0:np_, :], tvar[0:np_, :], Act.Ln)
                sc.drain()
                P.inc(sc.activation(rstd2[0:np_, :], lvar[0:np_, :], Act.Exp,
                                    scale=-0.5), "q1", 1)

            @blk.vector
            def _(v, c=c, s=s, mini=mini, np_=np_):
                xav = xamini[0:2, :] if mini else xa[:, s * 1024:(s + 1) * 1024]
                base = N_Q1 * c + 6 * s
                v.wait_ge(P.sems["q1"], base + 2)
                v.tensor_scalar(xn2f[0:np_, :], xav, negm[0:np_, 0:1],
                                rstd2[0:np_, 0:1], Alu.add, Alu.mult)
                v.drain()
                v.tensor_reduce(amax1[0:np_, :], xn2f[0:np_, :], axis=AX.X,
                                op=Alu.max, apply_absolute_value=True)
                v.drain()
                dst = amx1m[0:2, 0:1] if mini else amx1[:, s:s + 1]
                P.inc(v.tensor_scalar(dst, amax1[0:np_, :], 1e-5, None,
                                      Alu.max), "q1", 1)

            @blk.scalar
            def _(sc, c=c, s=s, mini=mini, np_=np_):
                base = N_Q1 * c + 6 * s
                sc.wait_ge(P.sems["q1"], base + 3)
                src = amx1m[0:2, 0:1] if mini else amx1[:, s:s + 1]
                sc.activation(la1[0:np_, :], src, Act.Ln)
                sc.drain()
                P.inc(sc.activation(sfac[0:np_, :], la1[0:np_, :], Act.Exp,
                                    scale=-1.0, bias=LN127), "q1", 1)

            @blk.vector
            def _(v, c=c, s=s, np_=np_):
                base = N_Q1 * c + 6 * s
                v.wait_ge(P.sems["q1"], base + 4)
                if s >= 2:
                    v.wait_ge(P.sems["trx"], N_TRX * c + 2 * (s - 1))
                if s == 4:
                    v.wait_ge(P.sems["trx"], N_TRX * c + 6)
                P.inc(v.tensor_scalar(xq1b[s % 2][0:np_, :], xn2f[0:np_, :],
                                      sfac[0:np_, 0:1], None, Alu.mult),
                      "q1", 1)

            @blk.scalar
            def _(sc, c=c, s=s, mini=mini, np_=np_):
                base = N_Q1 * c + 6 * s
                sc.wait_ge(P.sems["q1"], base + 5)
                sc.activation(xn2f[0:np_, :], xq1b[s % 2][0:np_, :], Act.Copy,
                              bias=MAGIC)
                sc.drain()
                P.inc(sc.activation(xq1b[s % 2][0:np_, :], xn2f[0:np_, :],
                                    Act.Copy, bias=-MAGIC), "q1", 1)

            @blk.scalar
            def _(sc, c=c, s=s, mini=mini):
                if mini:
                    sc.wait_ge(P.sems["trx"], N_TRX * (c + 1))
                    for k in range(8):
                        a = sc.activation(xts(k, 512, 514),
                                          ptb[:, 2 * k:2 * k + 2], Act.Copy)
                    P.inc(a, "trev", 1)
                else:
                    import os as _os
                    for half in range(2):
                        bi = N_TRX * c + 2 * s + half
                        sc.wait_ge(P.sems["trx"], bi + 1)
                        if _os.environ.get("MKDBG") and c == 0 and s == 0 \
                                and half == 0:
                            def _cb(pt, xq):
                                pt = np.asarray(pt).astype(np.float32)
                                xq = np.asarray(xq).astype(np.float32)
                                print("DBG ptb nan:", np.isnan(pt).mean(),
                                      "absmax:", np.nanmax(np.abs(pt)),
                                      "xq nan:", np.isnan(xq).mean())
                            sc.dbg_aps(_cb, ptb[:, 0:128],
                                       xq1b[0][:, 0:128])
                        for j in range(4):
                            k = half * 4 + j
                            a = sc.activation(xts(k, s * 128, (s + 1) * 128),
                                              ptb[:, j * 128:(j + 1) * 128],
                                              Act.Copy)
                        P.inc(a, "trev", 1)

        @blk.tensor
        def _(te, c=c):
            te.wait_ge(P.sems["efw"], 64)
            for s in range(4):
                for half in range(2):
                    bi = N_TRX * c + 2 * s + half
                    te.wait_ge(P.sems["q1"], N_Q1 * c + 6 * s + 6)
                    if bi > 0:
                        te.wait_ge(P.sems["trev"], bi)
                    import os as _os
                    for j in range(4):
                        k = half * 4 + j
                        mm = te.transpose(ptb[:, j * 128:(j + 1) * 128],
                                          xq1b[s % 2][:, k * 128:(k + 1) * 128],
                                          identf2[:])
                        if _os.environ.get("MKDBG") and c == 0 and s == 0 \
                                and half == 0:
                            def _cbj(pt, j=j):
                                pt = np.asarray(pt).astype(np.float32)
                                print(f"DBG-j{j} ptb[0:128] nan:",
                                      np.isnan(pt).mean())
                            te.dbg_aps(_cbj, ptb[:, 0:128])
                    if _os.environ.get("MKDBG") and c == 0 and s == 0 \
                            and half == 0:
                        def _cb2(pt, idn, xq):
                            pt = np.asarray(pt).astype(np.float32)
                            idn = np.asarray(idn).astype(np.float32)
                            xq = np.asarray(xq).astype(np.float32)
                            print("DBG-TE ptb nan:", np.isnan(pt).mean(),
                                  "ident diagsum:", np.nansum(idn),
                                  "offdiag:", np.nansum(np.abs(idn - np.eye(128))),
                                  "xq nan:", np.isnan(xq).mean())
                        te.dbg_aps(_cb2, ptb[:, 0:128], identb2[:],
                                   xq1b[0][:, 0:128])
                    P.inc(mm, "trx", 1)
            te.wait_ge(P.sems["q1"], N_Q1 * (c + 1))
            te.wait_ge(P.sems["trev"], N_TRX * c + 8)
            for k in range(8):
                mm = te.transpose(ptb[:, 2 * k:2 * k + 2],
                                  xq1b[0][0:2, k * 128:(k + 1) * 128],
                                  identf2[0:2, 0:2])
            P.inc(mm, "trx", 1)

        @blk.tensor
        def _(te, c=c):
            te.wait_ge(P.sems["q1"], N_Q1 * (c + 1))
            te.wait_ge(P.sems["efw"], 64)
            if c > 0:
                te.wait_ge(P.sems["s2"], N_S2 * c)
            for s in range(4):
                mm = te.matmul(pR[0:1, s * 128:(s + 1) * 128], amx1[:, s:s + 1],
                               identf2[:], start=True, stop=True)
            P.inc(mm, "rr", 1)

        @blk.scalar
        def _(sc, c=c):
            sc.wait_ge(P.sems["rr"], N_RR * c + 1)
            sc.wait_ge(P.sems["wsm"], 7)
            P.inc(sc.activation(rrow[0:1, 0:512], pR[0:1, 0:512], Act.Copy,
                                scale=mw127[0:1, 0:1]), "rr", 1)

        @blk.tensor
        def _(te, c=c):
            te.wait_ge(P.sems["rr"], N_RR * c + 2)
            P.inc(te.matmul(pR[0:1, 0:2], amx1m[0:2, 0:1], identf2[0:2, 0:2],
                            start=True, stop=True), "rr", 1)

        @blk.scalar
        def _(sc, c=c):
            sc.wait_ge(P.sems["rr"], N_RR * c + 3)
            P.inc(sc.activation(rrow[0:1, 512:514], pR[0:1, 0:2], Act.Copy,
                                scale=mw127[0:1, 0:1]), "rr", 1)

        @blk.tensor
        def _(te, c=c):
            te.wait_ge(P.sems["rr"], N_RR * c + 4)
            if c > 0:
                te.wait_ge(P.sems["s2"], N_S2 * c)
            P.inc(te.matmul(ptf[:, 0:512], onesr_f[:], rrow[0:1, 0:512],
                            start=True, stop=True), "rr", 1)

        @blk.vector
        def _(v, c=c):
            v.wait_ge(P.sems["rr"], N_RR * c + 5)
            if c > 0:
                v.wait_ge(P.sems["qs"], N_QS * c)
            P.inc(v.tensor_copy(rbig[:, 0:512], ptf[:, 0:512]), "rr", 1)

        @blk.tensor
        def _(te, c=c):
            te.wait_ge(P.sems["rr"], N_RR * c + 6)
            P.inc(te.matmul(ptf[:, 0:2], onesr_f[:], rrow[0:1, 512:514],
                            start=True, stop=True), "rr", 1)

        @blk.vector
        def _(v, c=c):
            v.wait_ge(P.sems["rr"], N_RR * c + 7)
            P.inc(v.tensor_copy(rbig[:, 512:514], ptf[:, 0:2]), "rr", 1)

        @blk.tensor
        def _(te, c=c):
            te.wait_ge(P.sems["trev"], N_TREV * (c + 1))
            te.wait_ge(P.sems["wqs"], 8)
            te.wait_ge(P.sems["rr"], N_RR * (c + 1))
            te.wait_ge(P.sems["xev"], N_XEV * (c + 1))
            for n4 in range(4):
                base = N_HLO * c + 3 * n4
                if base > 0:
                    te.wait_ge(P.sems["hlo"], base - 1)
                for k in range(8):
                    mm = te.matmul(
                        pE[:], xts(k, 512, 514),
                        w1q[:, k * 2048 + n4 * 512:k * 2048 + (n4 + 1) * 512],
                        start=(k == 0), stop=(k == 7))
                P.inc(mm, "hlo", 1)
                te.wait_ge(P.sems["hlo"], base + 2)
                for j in range(4):
                    mm = te.transpose(
                        ptf[:, (n4 * 4 + j) * 2:(n4 * 4 + j) * 2 + 2],
                        hE[0:2, j * 128:(j + 1) * 128], identf2[0:2, 0:2])
                P.inc(mm, "hlo", 1)

        @blk.tensor
        def _(te, c=c):
            te.wait_ge(P.sems["trev"], N_TREV * (c + 1))
            te.wait_ge(P.sems["wqs"], 8)
            if c == 0:
                te.wait_ge(P.sems["wqs"], 24)
            for i in range(16):
                pX = pA if i % 2 == 0 else pB
                if i >= 2:
                    te.wait_ge(P.sems["hba"], N_HBA * c + i - 1)
                for k in range(8):
                    mm = te.matmul(
                        pX[:],
                        w1q[:, k * 2048 + i * 128:k * 2048 + (i + 1) * 128],
                        xts(k, 0, 512), start=(k == 0), stop=(k == 7))
                P.inc(mm, "m1", 1)

        @blk.scalar
        def _(sc, c=c):
            for n4 in range(4):
                base = N_HLO * c + 3 * n4
                sc.wait_ge(P.sems["hlo"], base + 1)
                P.inc(sc.activation(hE[0:2, :], pE[0:2, :], Act.Copy), "hlo", 1)

        @blk.vector
        def _(v, c=c):
            v.wait_ge(P.sems["hlo"], N_HLO * c + 12)
            P.inc(v.tensor_copy(ptrh[:, 0:32], ptf[:, 0:32]), "hlo", 1)

        for i in range(16):
            @blk.vector
            def _(v, c=c, i=i):
                v.wait_ge(P.sems["m1"], N_M1 * c + i + 1)
                v.wait_ge(P.sems["rr"], N_RR * (c + 1))
                v.wait_ge(P.sems["hlo"], N_HLO * (c + 1))
                if c > 0:
                    v.wait_ge(P.sems["qs"], N_QS * c)
                pX = pA if i % 2 == 0 else pB
                v.tensor_tensor(hslot(i, 1, 513), pX[:], rbig[:, 0:512],
                                Alu.mult)
                hv = bass.AP(hb, i * 516, [[HBS, 128], [513, 2]])
                if c == 0:
                    v.tensor_tensor(hv, ptrh[:, 2 * i:2 * i + 2],
                                    rbig[:, 512:514], Alu.mult)
                    v.drain()
                    P.inc(v.memset(hslot(i, 0, 1), 0.0), "hba", 1)
                elif c == NCH - 1:
                    v.tensor_tensor(hv, ptrh[:, 2 * i:2 * i + 2],
                                    rbig[:, 512:514], Alu.mult)
                    v.drain()
                    P.inc(v.memset(hslot(i, 513, 514), 0.0), "hba", 1)
                else:
                    P.inc(v.tensor_tensor(hv, ptrh[:, 2 * i:2 * i + 2],
                                          rbig[:, 512:514], Alu.mult),
                          "hba", 1)

            @blk.vector
            def _(v, c=c, i=i):
                v.wait_ge(P.sems["hba"], N_HBA * c + i + 1)
                v.wait_ge(P.sems["efw"], 64)
                if i >= 1:
                    v.wait_ge(P.sems["cnv"], N_CNV * c + 3 * (i - 1) + 2)
                elif c > 0:
                    v.wait_ge(P.sems["cst"], 64 * c)
                w0a = cwsb[:, 3 * i:3 * i + 1]
                w1a = cwsb[:, 3 * i + 1:3 * i + 2]
                w2a = cwsb[:, 3 * i + 2:3 * i + 3]
                v.tensor_scalar(cbufA[:, 0:512], hslot(i, 1, 513), w1a, None,
                                Alu.mult)
                v.scalar_tensor_tensor(cbufB[:, 0:512], hslot(i, 0, 512), w0a,
                                       cbufA[:, 0:512], Alu.mult, Alu.add)
                P.inc(v.scalar_tensor_tensor(cbufA[:, 0:512],
                                             hslot(i, 2, 514), w2a,
                                             cbufB[:, 0:512], Alu.mult,
                                             Alu.add), "cnv", 1)

            @blk.scalar
            def _(sc, c=c, i=i):
                sc.wait_ge(P.sems["cnv"], N_CNV * c + 3 * i + 1)
                hs = hslot(16 if i == 0 else i - 1, 0, 512)
                sc.activation(hs, cbufA[:, 0:512], Act.Silu,
                              bias=cbsb[:, i:i + 1])
                sc.drain()
                P.inc(sc.activation(absb[:], hs, Act.Abs), "cnv", 1)

            @blk.vector
            def _(v, c=c, i=i):
                v.wait_ge(P.sems["cnv"], N_CNV * c + 3 * i + 2)
                if i == 0:
                    P.inc(v.tensor_copy(acc2[0][:], absb[:]), "cnv", 1)
                else:
                    P.inc(v.tensor_tensor(acc2[i % 2][:], acc2[(i + 1) % 2][:],
                                          absb[:], Alu.max), "cnv", 1)

        @blk.tensor
        def _(te, c=c):
            te.wait_ge(P.sems["cnv"], N_CNV * (c + 1))
            te.wait_ge(P.sems["hlo"], N_HLO * (c + 1))
            for j in range(4):
                mm = te.transpose(ptf[:, j * 128:(j + 1) * 128],
                                  acc2[1][:, j * 128:(j + 1) * 128],
                                  identf2[:])
            P.inc(mm, "s2", 1)

        @blk.vector
        def _(v, c=c):
            v.wait_ge(P.sems["s2"], N_S2 * c + 1)
            for j in range(4):
                a = v.tensor_reduce(amx_sb[:, j:j + 1],
                                    ptf[:, j * 128:(j + 1) * 128],
                                    axis=AX.X, op=Alu.max)
            P.inc(a, "s2", 1)

        @blk.gpsimd
        def _(g, c=c):
            g.wait_ge(P.sems["s2"], N_S2 * c + 2)
            g.dma_start(out=amx_i.ap(), in_=amx_sb[:]).then_inc(P.sems["amd"], 16)
            g.wait_ge(P.sems["amd"], 32 * c + 16)
            g.collective_compute(
                "AllReduce", Alu.max,
                replica_groups=[[0, 1], [2, 3], [4, 5], [6, 7]],
                ins=[amx_i.ap().opt()], outs=[amx_o.ap().opt()],
            ).then_inc(P.sems["arm"], 1)
            P.cnt["arm"] += 1
            g.wait_ge(P.sems["arm"], c + 1)
            g.dma_start(out=amx2_sb[:], in_=amx_o.ap()).then_inc(P.sems["amd"], 16)
            P.cnt["amd"] += 32

        @blk.vector
        def _(v, c=c):
            v.wait_ge(P.sems["amd"], 32 * (c + 1))
            v.wait_ge(P.sems["wsm"], 7)
            v.tensor_scalar(amax2c[:], amx2_sb[:], 1e-5, None, Alu.max)
            v.drain()
            P.inc(v.tensor_scalar(r2sb[:], amax2c[:], bc2[:, 0:1], None,
                                  Alu.mult), "s2", 1)

        @blk.scalar
        def _(sc, c=c):
            sc.wait_ge(P.sems["s2"], N_S2 * c + 3)
            sc.activation(s2l[:], amax2c[:], Act.Ln)
            sc.drain()
            P.inc(sc.activation(s2sb[:], s2l[:], Act.Exp, scale=-1.0,
                                bias=LN127), "s2", 1)

        @blk.tensor
        def _(te, c=c):
            te.wait_ge(P.sems["s2"], N_S2 * c + 4)
            for s in range(4):
                mm = te.matmul(pR[0:1, s * 128:(s + 1) * 128], s2sb[:, s:s + 1],
                               identf2[:], start=True, stop=True)
            P.inc(mm, "s2", 1)

        @blk.scalar
        def _(sc, c=c):
            sc.wait_ge(P.sems["s2"], N_S2 * c + 5)
            P.inc(sc.activation(s2row[0:1, :], pR[0:1, 0:512], Act.Copy),
                  "s2", 1)

        @blk.tensor
        def _(te, c=c):
            te.wait_ge(P.sems["s2"], N_S2 * c + 6)
            P.inc(te.matmul(ptf[:, 0:512], onesr_f[:], s2row[0:1, :],
                            start=True, stop=True), "s2", 1)

        @blk.vector
        def _(v, c=c):
            v.wait_ge(P.sems["s2"], N_S2 * c + 7)
            v.wait_ge(P.sems["hba"], N_HBA * (c + 1))
            P.inc(v.tensor_copy(s2big[:], ptf[:, 0:512]), "s2", 1)

        for i in range(16):
            @blk.vector
            def _(v, c=c, i=i):
                v.wait_ge(P.sems["s2"], N_S2 * (c + 1))
                if c == 0 and i == 0:
                    v.wait_ge(P.sems["wqs"], 24)
                if c > 0 and i == 0:
                    v.wait_ge(P.sems["m2"], N_M2 * c)
                if i >= 1:
                    v.wait_ge(P.sems["qs"], N_QS * c + 3 * (i - 1) + 2)
                hs = hslot(16 if i == 0 else i - 1, 0, 512)
                P.inc(v.tensor_tensor(cbufB[:, 0:512], hs, s2big[:], Alu.mult),
                      "qs", 1)

            @blk.scalar
            def _(sc, c=c, i=i):
                sc.wait_ge(P.sems["qs"], N_QS * c + 3 * i + 1)
                P.inc(sc.activation(cbufA[:, 0:512], cbufB[:, 0:512], Act.Copy,
                                    bias=MAGIC), "qs", 1)
                sc.drain()
                P.inc(sc.activation(h2q[:, i * 512:(i + 1) * 512],
                                    cbufA[:, 0:512], Act.Copy, bias=-MAGIC),
                      "qs", 1)

        @blk.tensor
        def _(te, c=c):
            te.wait_ge(P.sems["qs"], N_QS * (c + 1))
            te.wait_ge(P.sems["wqs"], 24)
            for s in range(4):
                if s >= 2:
                    te.wait_ge(P.sems["ctv"], N_CTV * c + s - 1)
                pX, pY = WOB[2 * (s % 2)], WOB[2 * (s % 2) + 1]
                for k in range(16):
                    te.matmul(pX[:],
                              h2q[:, k * 512 + s * 128:k * 512 + (s + 1) * 128],
                              w2q[:, k * 1024:k * 1024 + 512],
                              start=(k == 0), stop=(k == 15))
                    mm = te.matmul(
                        pY[:], h2q[:, k * 512 + s * 128:k * 512 + (s + 1) * 128],
                        w2q[:, k * 1024 + 512:(k + 1) * 1024],
                        start=(k == 0), stop=(k == 15))
                P.inc(mm, "m2", 1)

        @blk.vector
        def _(v, c=c):
            for s in range(4):
                v.wait_ge(P.sems["m2"], N_M2 * c + s + 1)
                v.wait_ge(P.sems["cst"], 16 * (4 * c + s))
                pX, pY = WOB[2 * (s % 2)], WOB[2 * (s % 2) + 1]
                v.tensor_scalar(cbufA[:, 0:512], pX[:], r2sb[:, s:s + 1],
                                None, Alu.mult)
                v.tensor_scalar(cbufA[:, 512:1024], pY[:], r2sb[:, s:s + 1],
                                None, Alu.mult)
                v.drain()
                v.scalar_tensor_tensor(cbufB[:, 0:512],
                                       xa[:, s * 1024:s * 1024 + 512],
                                       0.5, cbufA[:, 0:512], Alu.mult, Alu.add)
                P.inc(v.scalar_tensor_tensor(
                    cbufB[:, 512:1024], xa[:, s * 1024 + 512:(s + 1) * 1024],
                    0.5, cbufA[:, 512:1024], Alu.mult, Alu.add), "ctv", 1)

        @blk.gpsimd
        def _(g, c=c, t0=t0):
            for s in range(4):
                g.wait_ge(P.sems["ctv"], N_CTV * c + s + 1)
                g.dma_start(out=rs_in.ap()[t0 + s * 128:t0 + (s + 1) * 128, :],
                            in_=cbufB[:]).then_inc(P.sems["cst"], 16)
                P.cnt["cst"] += 16

    @blk.gpsimd
    def _(g):
        g.wait_ge(P.sems["cst"], 16 * 4 * NCH)
        g.collective_compute(
            "ReduceScatter", Alu.add,
            replica_groups=[[0, 1], [2, 3], [4, 5], [6, 7]],
            ins=[rs_in.ap().opt()], outs=[rs_out.ap().opt()],
        ).then_inc(P.sems["rsd"], 1)
        P.cnt["rsd"] += 1
        g.wait_ge(P.sems["rsd"], 1)
        g.dma_start(out=out_ap, in_=rs_out.ap()).then_inc(P.sems["cst"], 16)
        P.cnt["cst"] += 16
        g.wait_ge(P.sems["cst"], 16 * 4 * NCH + 16)

    return dict(rs_in=rs_in, cc_out=cc_out)


def _build_program():
    P = Prog()
    nc = P.nc
    for name, shape, dt in [("xb", (T, D), F32), ("xTt", (32, D, 128), F32),
                            ("wqkv", (D, COLS), F32), ("bqkv", (1, COLS), F32),
                            ("C2", (9, TH, 128), F32), ("S2n", (9, TH, 128), F32),
                            ("twc", (128, 9), F32), ("tws", (128, 9), F32),
                            ("twns", (128, 9), F32), ("fioAB", (128, 34), F32),
                            ("jp1", (128, 512), F32), ("LT", (17, FPAD, 128), BF16),
                            ("KR", (17, 128, 576), F32), ("cvsv", (2, T), F32),
                            ("cva", (2, T), F32), ("cvb", (2, T), F32),
                            ("ident", (128, 128), BF16), ("identf", (128, 128), F32),
                            ("qmw", (128, 100), BF16),
                            ("wo", (D, D), F32), ("bo", (1, D), F32),
                            ("w1t", (1024, 2048), F32), ("w2t", (2048, 1024), F32),
                            ("cwh", (128, 48), F32), ("cbh", (128, 16), F32)]:
        P.din(name, shape, dt)
    yout = P.nc.dram_tensor("yout", [TH, D], F32, kind="ExternalOutput")
    with P.es:
        with P.nc.Block() as blk:
            with contextlib.ExitStack() as scA:
                stA = emit_ln_proj(P, blk, scA)
            with contextlib.ExitStack() as scB:
                stB = emit_dft(P, blk, scB, stA["q_sb"])
            with contextlib.ExitStack() as scC:
                stC = emit_topk(P, blk, scC, stB["energy"], stA["q_sb"])
            with contextlib.ExitStack() as scD:
                stD = emit_attn_tail(P, blk, scD, stA, stB, stC)
            with contextlib.ExitStack() as scE:
                emit_ef(P, blk, scE, stD["ot_sb"], yout.ap())
    return P


def kernel(**inputs):
    P = _build_program()
    ins = [stage_core(inputs, core) for core in range(8)]
    res = run_bass_kernel_spmd(P.nc, ins, core_ids=list(range(8)))
    out = np.zeros((4, T, D), np.float32)
    for c in range(8):
        b, half = c // 2, c % 2
        out[b, half * TH:(half + 1) * TH] = np.asarray(res.results[c]["yout"])
    return out
